# version 01
# baseline (speedup 1.0000x reference)
import math
from contextlib import ExitStack

import numpy as np
import ml_dtypes

import concourse.bass as bass
import concourse.mybir as mybir
from concourse.bass_utils import run_bass_kernel_spmd

F32 = mybir.dt.float32
BF16 = mybir.dt.bfloat16
AF = mybir.ActivationFunctionType
ALU = mybir.AluOpType
AX = mybir.AxisListType

NCORES = 8
B, S, D = 16, 2048, 1024
H, HD = 16, 64
NT = S // 128
LAMBDA_INIT = 0.8 - 0.6 * math.exp(-0.3 * 0)
BF16NP = ml_dtypes.bfloat16


class Sem:
    def __init__(self, handle):
        self.h = handle
        self.n = 0

    def inc(self, instr, k=1):
        instr.then_inc(self.h, k)
        self.n += k
        return instr


class DStream:

    def __init__(self, ctx, name, total=None):
        self.s = [ctx.sem(name + "a"), ctx.sem(name + "b")]
        self.n_issued = 0
        self.total = total

    def issue(self, instr, k=16):
        assert k == 16
        p = self.n_issued % 2
        instr.then_inc(self.s[p].h, 16)
        self.n_issued += 1
        return instr

    def wait_item(self, eng, i):
        eng.wait_ge(self.s[i % 2].h, 16 * (i // 2 + 1))

    def wait_all(self, eng):
        n = self.total if self.total is not None else self.n_issued
        assert n > 0
        for p in range(2):
            cnt = 16 * ((n - p + 1) // 2)
            if cnt:
                eng.wait_ge(self.s[p].h, cnt)


class Chain:

    def __init__(self, ctx, name):
        self.s = ctx.sem(name)

    def edge(self, eng, instr):
        self.s.inc(instr)
        eng.wait_ge(self.s.h, self.s.n)
        return instr


class Ctx:
    def __init__(self, nc, block, es):
        self.nc, self.block = nc, block
        self.es = es
        self.stack = [es]
        self._id = 0

    def push(self):
        sub = ExitStack()
        self.stack.append(sub)
        return sub

    def pop(self):
        self.stack.pop().close()

    def sem(self, name):
        self._id += 1
        return Sem(self.es.enter_context(self.nc.semaphore(f"{name}_{self._id}")))

    def dstream(self, name):
        return DStream(self, name)

    def chain(self, name):
        return Chain(self, name)

    def sbuf(self, name, shape, dtype):
        self._id += 1
        return self.stack[-1].enter_context(
            self.nc.sbuf_tensor(f"{name}_{self._id}", list(shape), dtype))

    def psum(self, name, shape, dtype=F32):
        self._id += 1
        return self.stack[-1].enter_context(
            self.nc.psum_tensor(f"{name}_{self._id}", list(shape), dtype))

    def const(self, name, arr):
        self._id += 1
        return self.nc.inline_tensor(np.ascontiguousarray(arr), f"{name}_{self._id}")

    def barrier(self):
        bs = self.sem("bar")
        block = self.block
        for starter in (block.sync, block.gpsimd, block.scalar, block.vector, block.tensor):
            @starter
            def _(eng):
                eng.sem_inc(bs.h, 1)
                eng.wait_ge(bs.h, 5)



def host_prepare(inp):
    a = lambda k: np.asarray(inp[k], np.float32)
    d = {}
    d["a0_WqkvT"] = np.ascontiguousarray(a("L0_attn_Wqkv").T)
    d["a0_WprojTb"] = np.ascontiguousarray(
        np.concatenate([a("L0_attn_Wproj").T, a("L0_attn_bproj")[None, :]], 0))
    d["f0_Wg"] = np.ascontiguousarray(a("L0_ffn_Wg"))
    WqcT = np.transpose(a("L0_ffn_Wq"), (1, 0, 2)).reshape(D, H * 64)
    d["f0_WqcTb"] = np.ascontiguousarray(
        np.concatenate([WqcT, a("L0_ffn_bq").reshape(1, H * 64)], 0))
    Wkv3 = np.transpose(a("L0_ffn_Wkv"), (1, 0, 2))
    WkvcT = np.concatenate([Wkv3[:, :, 0:64].reshape(D, H * 64),
                            Wkv3[:, :, 64:128].reshape(D, H * 64)], 1)
    bkv2 = a("L0_ffn_bkv")
    bkv_row = np.concatenate([bkv2[:, 0:64].reshape(1, H * 64),
                              bkv2[:, 64:128].reshape(1, H * 64)], 1)
    d["f0_WkvcTb"] = np.ascontiguousarray(np.concatenate([WkvcT, bkv_row], 0))
    d["f0_WoS"] = np.ascontiguousarray(a("L0_ffn_Wo").reshape(H * 64, D))
    d["f0_bo_rep"] = np.ascontiguousarray(np.broadcast_to(a("L0_ffn_bo")[None, :], (128, D)))
    d["f0_lng_rep"] = np.ascontiguousarray(np.broadcast_to(a("L0_ffn_ln_g")[None, :], (128, D)))
    d["f0_lnb_rep"] = np.ascontiguousarray(np.broadcast_to(a("L0_ffn_ln_b")[None, :], (128, D)))
    d["a1_WqTb"] = np.ascontiguousarray(
        np.concatenate([a("L1_attn_Wq").T, a("L1_attn_bq")[None, :]], 0))
    d["a1_WkTb"] = np.ascontiguousarray(
        np.concatenate([a("L1_attn_Wk").T, a("L1_attn_bk")[None, :]], 0))
    d["a1_WvTb"] = np.ascontiguousarray(
        np.concatenate([a("L1_attn_Wv").T, a("L1_attn_bv")[None, :]], 0))
    d["a1_WoT"] = np.ascontiguousarray(a("L1_attn_Wo").T)
    d["a1_bo_row"] = np.ascontiguousarray(a("L1_attn_bo")[None, :])
    d["a1_hnw_col"] = np.ascontiguousarray(a("L1_attn_head_norm_w").reshape(H * HD, 1))
    for nm in ("lq1", "lk1", "lq2", "lk2"):
        d[f"a1_{nm}T"] = np.ascontiguousarray(a(f"L1_attn_{nm}").T)
    d["f1_W1"] = np.ascontiguousarray(a("L1_ffn_W1"))
    d["f1_b1cols"] = np.ascontiguousarray(
        np.transpose(a("L1_ffn_b1").reshape(8, 32, 128), (0, 2, 1)))
    d["f1_W2"] = np.ascontiguousarray(a("L1_ffn_W2"))
    d["f1_b2rows"] = np.ascontiguousarray(a("L1_ffn_b2")[:, None, :])
    d["f1_gateW"] = np.ascontiguousarray(a("L1_ffn_gate_W"))
    d["f1_sharedWb"] = np.ascontiguousarray(
        np.concatenate([a("L1_ffn_shared_W"), a("L1_ffn_shared_b")[None, :]], 0))
    d["f1_gam_rep"] = np.ascontiguousarray(np.broadcast_to(a("L1_ffn_gamma")[None, :], (128, D)))
    d["f1_bet_rep"] = np.ascontiguousarray(np.broadcast_to(a("L1_ffn_beta")[None, :], (128, D)))
    return d


def rope_tables():
    half = HD // 2
    inv32 = (1.0 / (10000.0 ** (np.arange(half, dtype=np.float32) / np.float32(half)))).astype(np.float32)
    ang = (np.arange(S, dtype=np.float32)[:, None] * inv32[None, :]).astype(np.float32)
    return np.cos(ang).astype(np.float32), np.sin(ang).astype(np.float32)


IOTA8E = np.ascontiguousarray(np.broadcast_to(8.0 - np.arange(8, dtype=np.float32), (128, 8)))
IOTA16B = np.ascontiguousarray(
    np.broadcast_to((16.0 - np.arange(16, dtype=np.float32))[:, None], (128, 16, 16)).reshape(128, 256))
IDENT_BF = np.eye(128, dtype=np.float32).astype(BF16NP)
MOE_NEXP = 8
DBG_UT = False
A0_STOP = None
A0_NMM = 1



class LNPipe:

    def __init__(self, ctx, name, eps=1e-5, bufs=None):
        if bufs is None:
            self.stat = ctx.sbuf(name + "st", [128, 2, 8], F32)
            self.sq_t = ctx.sbuf(name + "sq", [128, D], BF16)
            self.u1t = ctx.sbuf(name + "u1", [128, D], F32)
        else:
            self.stat, self.sq_t, self.u1t = bufs
        self.s_sq = ctx.sem(name + "sq")
        self.s_mu = ctx.sem(name + "mu")
        self.vch = ctx.chain(name + "vc")
        self.sch = ctx.chain(name + "sc")
        self.eps = eps

    def bufs(self):
        return (self.stat, self.sq_t, self.u1t)

    def sq(self, s, i, x_ap):
        pb = i % 2
        self.s_sq.inc(s.activation(self.sq_t[:], x_ap, AF.Square,
                                   accum_out=self.stat[:, pb, 1:2]))

    def stats(self, v, i, x_ap):
        pb = i % 2
        st = self.stat
        ch = self.vch
        v.wait_ge(self.s_sq.h, i + 1)
        ch.edge(v, v.reduce_sum(st[:, pb, 0:1], x_ap, axis=AX.X))
        ch.edge(v, v.tensor_scalar_mul(st[:, pb, 2:3], st[:, pb, 0:1], 1.0 / D))
        v.tensor_scalar_mul(st[:, pb, 6:7], st[:, pb, 2:3], -1.0)
        ch.edge(v, v.tensor_tensor(st[:, pb, 3:4], st[:, pb, 2:3], st[:, pb, 2:3], ALU.mult))
        ch.edge(v, v.scalar_tensor_tensor(st[:, pb, 3:4], st[:, pb, 1:2], 1.0 / D,
                                          st[:, pb, 3:4], ALU.mult, ALU.subtract))
        self.s_mu.inc(v.tensor_scalar_add(st[:, pb, 3:4], st[:, pb, 3:4], self.eps))

    def norm(self, s, i, x_ap, out_ap, s_done):
        pb = i % 2
        st = self.stat
        ch = self.sch
        s.wait_ge(self.s_mu.h, i + 1)
        ch.edge(s, s.activation(st[:, pb, 4:5], st[:, pb, 3:4], AF.Ln))
        ch.edge(s, s.activation(st[:, pb, 5:6], st[:, pb, 4:5], AF.Exp, scale=-0.5))
        s.activation(self.u1t[:], x_ap, AF.Identity, bias=st[:, pb, 6:7])
        s_done.inc(s.activation(out_ap, self.u1t[:], AF.Identity,
                                scale=st[:, pb, 5:6]))



def emit_moe(ctx, io, tag):
    nc, block = ctx.nc, ctx.block
    T = 4096
    NTT = T // 128
    NTB = T // 512

    x_in, x_out = io["x_in"], io["x_out"]
    uT_dram, outacc = io["uT"], io["outacc"]
    x_in_t = x_in.rearrange("(n p) d -> n p d", p=128)
    x_out_t = x_out.rearrange("(n p) d -> n p d", p=128)
    outacc_t = outacc.rearrange("(n p) d -> n p d", p=128)
    outacc_b = outacc.rearrange("(tb s p) d -> tb p s d", p=128, s=4)

    c_sb = ctx.sbuf("comb" + tag, [128, NTT, 8], F32)
    ones_sb = ctx.sbuf("ones" + tag, [1, 128], BF16)
    gam_sb = ctx.sbuf("gam" + tag, [128, D], F32)
    bet_sb = ctx.sbuf("bet" + tag, [128, D], F32)

    onesrow_c = ctx.const("onesr" + tag, np.ones((1, 128), np.float32).astype(BF16NP))
    id_c = ctx.const("idbf" + tag, IDENT_BF)
    iota_c = ctx.const("iota8" + tag, IOTA8E)

    s_cst = ctx.sem("cst")
    d_x = ctx.dstream("xld")
    s_u = ctx.sem("ubf")
    s_tr = ctx.sem("tr")
    s_te = ctx.sem("te")
    d_ud = ctx.dstream("ud")
    s_gg = ctx.sem("gg")
    s_gm = ctx.sem("gm")
    s_gx = ctx.sem("gx")
    s_cw = ctx.sem("cw")

    ctx.push()
    gateW_sb = ctx.sbuf("gateW" + tag, [128, 8, 8], BF16)
    iota_sb = ctx.sbuf("iota" + tag, [128, 8], F32)
    id_sb = ctx.sbuf("id" + tag, [128, 128], BF16)
    xt = [ctx.sbuf("xt" + tag, [128, D], F32) for _ in range(2)]
    ub = [ctx.sbuf("ub" + tag, [128, D], BF16) for _ in range(2)]
    uTs = [ctx.sbuf("uTs" + tag, [128, 8, 128], BF16) for _ in range(2)]
    gsb = ctx.sbuf("gsb" + tag, [128, 8], F32)
    gscr = ctx.sbuf("gscr" + tag, [128, 24], F32)
    ps_tr = [ctx.psum("pstr" + tag, [128, 128], BF16) for _ in range(2)]
    ps_g = ctx.psum("psg" + tag, [128, 8], F32)
    ln1 = LNPipe(ctx, "mln" + tag)
    vch = ctx.chain("gch" + tag)

    @block.gpsimd
    def _(gp):
        s_cst.inc(gp.dma_start(out=gateW_sb[:], in_=io["gateW"].rearrange("(c p) e -> p c e", p=128)), 16)
        s_cst.inc(gp.dma_start(out=id_sb[:], in_=id_c.ap()), 16)
        s_cst.inc(gp.dma_start(out=iota_sb[:], in_=iota_c.ap()), 16)
        s_cst.inc(gp.dma_start(out=ones_sb[:], in_=onesrow_c.ap()), 16)
        s_cst.inc(gp.dma_start(out=gam_sb[:], in_=io["gam_rep"]), 16)
        s_cst.inc(gp.dma_start(out=bet_sb[:], in_=io["bet_rep"]), 16)

    @block.sync
    def _(sync):
        for t in range(NTT + 1):
            if t < NTT:
                if t >= 2:
                    sync.wait_ge(s_u.h, t - 1)
                d_x.issue(sync.dma_start(out=xt[t % 2][:], in_=x_in_t[t]))
            if t >= 1:
                j = t - 1
                sync.wait_ge(s_te.h, 8 * (j + 1))
                d_ud.issue(sync.dma_start(
                    out=uT_dram.rearrange("c p (n q) -> n p c q", q=128)[j],
                    in_=uTs[j % 2][:]))

    @block.scalar
    def _(s):
        for t in range(NTT + 2):
            if t < NTT:
                d_x.wait_item(s, t)
                ln1.sq(s, t, xt[t % 2][:])
            if 1 <= t <= NTT:
                j = t - 1
                if j >= 2:
                    s.wait_ge(s_tr.h, 8 * (j - 1))
                ln1.norm(s, j, xt[j % 2][:], ub[j % 2][:], s_u)
            if t >= 2:
                j2 = t - 2
                s.wait_ge(s_gm.h, j2 + 1)
                pn = j2 % 2
                s_gx.inc(s.activation(gsb[:], ps_g[:], AF.Exp, bias=gscr[:, pn:pn + 1]))

    @block.vector
    def _(v):
        for t in range(NTT + 2):
            if t >= 2:
                j2 = t - 2
                E = lambda ins: vch.edge(v, ins)
                v.wait_ge(s_gx.h, j2 + 1)
                E(v.reduce_sum(gscr[:, 2:3], gsb[:], axis=AX.X))
                E(v.reciprocal(gscr[:, 3:4], gscr[:, 2:3]))
                E(v.tensor_scalar_mul(gsb[:], gsb[:], gscr[:, 3:4]))
                E(v.reduce_max(gscr[:, 4:5], gsb[:], axis=AX.X))
                E(v.tensor_scalar(gscr[:, 8:16], gsb[:], gscr[:, 4:5], None, ALU.is_equal))
                E(v.tensor_tensor(gscr[:, 8:16], gscr[:, 8:16], iota_sb[:], ALU.mult))
                E(v.reduce_max(gscr[:, 5:6], gscr[:, 8:16], axis=AX.X))
                E(v.tensor_scalar(gscr[:, 8:16], gscr[:, 8:16], gscr[:, 5:6], None, ALU.is_equal))
                E(v.tensor_scalar(gscr[:, 16:24], gscr[:, 8:16], -1.0, 1.0, ALU.mult, ALU.add))
                E(v.tensor_tensor(gscr[:, 16:24], gscr[:, 16:24], gsb[:], ALU.mult))
                E(v.reduce_max(gscr[:, 6:7], gscr[:, 16:24], axis=AX.X))
                E(v.tensor_scalar(gscr[:, 16:24], gscr[:, 16:24], gscr[:, 6:7], None, ALU.is_equal))
                E(v.tensor_tensor(gscr[:, 16:24], gscr[:, 16:24], iota_sb[:], ALU.mult))
                E(v.reduce_max(gscr[:, 7:8], gscr[:, 16:24], axis=AX.X))
                E(v.tensor_scalar(gscr[:, 16:24], gscr[:, 16:24], gscr[:, 7:8], None, ALU.is_equal))
                E(v.tensor_tensor(gscr[:, 2:3], gscr[:, 4:5], gscr[:, 6:7], ALU.add))
                E(v.reciprocal(gscr[:, 3:4], gscr[:, 2:3]))
                E(v.tensor_scalar_mul(gscr[:, 8:16], gscr[:, 8:16], gscr[:, 4:5]))
                E(v.scalar_tensor_tensor(gscr[:, 8:16], gscr[:, 16:24], gscr[:, 6:7],
                                         gscr[:, 8:16], ALU.mult, ALU.add))
                s_cw.inc(v.tensor_scalar_mul(c_sb[:, j2, :], gscr[:, 8:16], gscr[:, 3:4]))
            if t < NTT:
                d_x.wait_item(v, t)
                ln1.stats(v, t, xt[t % 2][:])
            if 1 <= t <= NTT:
                j = t - 1
                for c in range(8):
                    g = 8 * j + c
                    v.wait_ge(s_tr.h, g + 1)
                    if j >= 2 and c == 0:
                        d_ud.wait_item(v, j - 2)
                    s_te.inc(v.tensor_copy(uTs[j % 2][:, c, :], ps_tr[g % 2][:]))
                v.wait_ge(s_gg.h, j + 1)
                pn = j % 2
                vch.edge(v, v.reduce_max(gscr[:, 2:3], ps_g[:], axis=AX.X))
                s_gm.inc(v.tensor_scalar_mul(gscr[:, pn:pn + 1], gscr[:, 2:3], -1.0))

    if "c_dbg" in io:
        s_cdbg = ctx.sem("cdbg")
        @block.sync
        def _(sync):
            sync.wait_ge(s_cw.h, NTT)
            sync.dma_start(out=io["c_dbg"], in_=c_sb[:]).then_inc(s_cdbg.h, 16)
            sync.wait_ge(s_cdbg.h, 16)

    @block.tensor
    def _(t):
        t.wait_ge(s_cst.h, 16 * 6)
        for i in range(NTT):
            pb = i % 2
            t.wait_ge(s_u.h, i + 1)
            for c in range(8):
                g = 8 * i + c
                if g >= 2:
                    t.wait_ge(s_te.h, g - 1)
                s_tr.inc(t.transpose(ps_tr[g % 2][:], ub[pb][:, c * 128:(c + 1) * 128], id_sb[:]))
            t.wait_ge(s_te.h, 8 * (i + 1))
            if i >= 1:
                t.wait_ge(s_gx.h, i)
            for c in range(8):
                mm = t.matmul(ps_g[:], uTs[pb][:, c, :], gateW_sb[:, c, :],
                              start=(c == 0), stop=(c == 7))
            s_gg.inc(mm)

    ctx.pop()
    PH1 = [(s_gx, NTT), (s_cw, NTT), (s_gg, NTT), (s_cst, 16 * 6)]

    def ph1_waits(eng):
        for _s, _n in PH1:
            eng.wait_ge(_s.h, _n)
        d_ud.wait_all(eng)

    ctx.push()
    w1_sb = ctx.sbuf("w1" + tag, [128, 8, 4096], BF16)
    b1c_sb = ctx.sbuf("b1c" + tag, [128, 32], F32)
    w2_sb = ctx.sbuf("w2" + tag, [128, 32, 1024], BF16)
    b2r_sb = ctx.sbuf("b2r" + tag, [1, 1024], BF16)
    uTb = [ctx.sbuf("uTb" + tag, [128, 8, 512], BF16) for _ in range(2)]
    h_sb = ctx.sbuf("h" + tag, [128, 32, 512], BF16)
    z_sb = ctx.sbuf("z" + tag, [128, 4, 1024], F32)
    ps_h = [ctx.psum("psh" + tag, [128, 512], F32) for _ in range(2)]
    ps_z = [ctx.psum("psz" + tag, [128, 512], F32) for _ in range(2)]

    s_wl = ctx.sem("wl")
    d_ul = ctx.dstream("ul")
    s_hm = ctx.sem("hm")
    s_hg = ctx.sem("hg")
    s_zm = ctx.sem("zm")
    s_ze = ctx.sem("ze")
    d_ow = ctx.dstream("ow")

    EIS = list(range(-1, MOE_NEXP))
    HK = 32

    hm_count, zm_count, hg_count, ze_count = {}, {}, {}, {}
    n_hm = n_zm = n_hg = n_ze = 0
    for ei in EIS:
        for tb in range(NTB):
            if ei >= 0:
                n_hm += HK
                n_hg += HK
            n_zm += 8
            n_ze += 8
            hm_count[(ei, tb)] = n_hm
            hg_count[(ei, tb)] = n_hg
            zm_count[(ei, tb)] = n_zm
            ze_count[(ei, tb)] = n_ze
    wl_after_shared = 9 * 16
    WL_PER_EXP = 42
    wl_count = {}
    nwl = wl_after_shared
    for ei in range(MOE_NEXP):
        nwl += WL_PER_EXP * 16
        wl_count[ei] = nwl

    @block.gpsimd
    def _(gp):
        ph1_waits(gp)
        W1r = io["W1"].rearrange("e (c p) f -> e c p f", p=128)
        W2r = io["W2"].rearrange("e (c p) o -> e c p o", p=128)
        SWr = io["sharedWb"]
        for c in range(8):
            gp.dma_start(out=w1_sb[:, c, 0:1024],
                         in_=SWr[0:1024, :].rearrange("(c p) o -> c p o", p=128)[c]).then_inc(s_wl.h, 16)
        gp.dma_start(out=b2r_sb[:], in_=SWr[1024:1025, :]).then_inc(s_wl.h, 16)
        s_wl.n = 9 * 16
        widx = 0
        for ei in EIS:
            if ei >= 0:
                prev = ei - 1
                gp.wait_ge(s_hm.h if prev >= 0 else s_zm.h,
                           hm_count[(prev, NTB - 1)] if prev >= 0 else zm_count[(-1, NTB - 1)])
                if prev >= 0:
                    gp.wait_ge(s_hg.h, hg_count[(prev, NTB - 1)])
                for c in range(8):
                    gp.dma_start(out=w1_sb[:, c, :], in_=W1r[ei, c]).then_inc(s_wl.h, 16)
                gp.dma_start(out=b1c_sb[:], in_=io["b1cols"][ei]).then_inc(s_wl.h, 16)
                gp.wait_ge(s_zm.h, zm_count[(prev, NTB - 1)])
                for c in range(32):
                    gp.dma_start(out=w2_sb[:, c, :], in_=W2r[ei, c]).then_inc(s_wl.h, 16)
                gp.dma_start(out=b2r_sb[:], in_=io["b2rows"][ei]).then_inc(s_wl.h, 16)
                s_wl.n += WL_PER_EXP * 16
            for tb in range(NTB):
                gp.wait_ge(s_ze.h, ze_count[(ei, tb)])
                if widx >= NTB:
                    d_ow.wait_item(gp, widx - NTB)
                d_ow.issue(gp.dma_start(out=outacc_b[tb], in_=z_sb[:],
                                        accum_op=(ALU.add if ei >= 0 else ALU.bypass)))
                widx += 1

    @block.sync
    def _(sync):
        ph1_waits(sync)
        uTb_dram = uT_dram.rearrange("c p (tb q) -> tb p c q", q=512)
        nul = 0
        for ei in EIS:
            for tb in range(NTB):
                if nul >= 2:
                    eip, tbp = EIS[(nul - 2) // NTB], (nul - 2) % NTB
                    sync.wait_ge(s_zm.h, zm_count[(eip, tbp)])
                d_ul.issue(sync.dma_start(out=uTb[nul % 2][:], in_=uTb_dram[tb]))
                nul += 1

    @block.tensor
    def _(t):
        ph1_waits(t)
        nul = 0
        for ei in EIS:
            for tb in range(NTB):
                pb = nul % 2
                d_ul.wait_item(t, nul)
                nul += 1
                if ei < 0:
                    t.wait_ge(s_wl.h, 9 * 16)
                    for half in range(2):
                        for sub in range(4):
                            zi = zm_count[(ei, tb)] - 8 + half * 4 + sub
                            if zi >= 2:
                                t.wait_ge(s_ze.h, zi - 1)
                            for c in range(8):
                                t.matmul(ps_z[zi % 2][:],
                                         uTb[pb][:, c, sub * 128:(sub + 1) * 128],
                                         w1_sb[:, c, half * 512:(half + 1) * 512],
                                         start=(c == 0), stop=False)
                            s_zm.inc(t.matmul(ps_z[zi % 2][:], ones_sb[:],
                                              b2r_sb[:, half * 512:(half + 1) * 512],
                                              start=False, stop=True))
                else:
                    t.wait_ge(s_wl.h, wl_count[ei] - 33 * 16)
                    for hk in range(HK):
                        hi = hm_count[(ei, tb)] - HK + hk
                        if hi >= 2:
                            t.wait_ge(s_hg.h, hi - 1)
                        for c in range(8):
                            mm = t.matmul(ps_h[hi % 2][:],
                                          w1_sb[:, c, hk * 128:(hk + 1) * 128],
                                          uTb[pb][:, c, :], start=(c == 0), stop=(c == 7))
                        s_hm.inc(mm)
                    t.wait_ge(s_wl.h, wl_count[ei])
                    t.wait_ge(s_hg.h, hg_count[(ei, tb)])
                    for half in range(2):
                        for sub in range(4):
                            zi = zm_count[(ei, tb)] - 8 + half * 4 + sub
                            if zi >= 2:
                                t.wait_ge(s_ze.h, zi - 1)
                            for hk in range(HK):
                                t.matmul(ps_z[zi % 2][:],
                                         h_sb[:, hk, sub * 128:(sub + 1) * 128],
                                         w2_sb[:, hk, half * 512:(half + 1) * 512],
                                         start=(hk == 0), stop=False)
                            s_zm.inc(t.matmul(ps_z[zi % 2][:], ones_sb[:],
                                              b2r_sb[:, half * 512:(half + 1) * 512],
                                              start=False, stop=True))

    @block.scalar
    def _(s):
        ph1_waits(s)
        widx = 0
        prev_blk = None
        for ei in EIS:
            for tb in range(NTB):
                if ei >= 0:
                    if prev_blk is not None:
                        s.wait_ge(s_zm.h, zm_count[prev_blk])
                    for hk in range(HK):
                        hi = hm_count[(ei, tb)] - HK + hk
                        s.wait_ge(s_hm.h, hi + 1)
                        s_hg.inc(s.activation(h_sb[:, hk, :], ps_h[hi % 2][:], AF.Gelu,
                                              bias=b1c_sb[:, hk:hk + 1]))
                if widx >= 1:
                    d_ow.wait_item(s, widx - 1)
                for half in range(2):
                    for sub in range(4):
                        zi = zm_count[(ei, tb)] - 8 + half * 4 + sub
                        s.wait_ge(s_zm.h, zi + 1)
                        dst = z_sb[:, sub, half * 512:(half + 1) * 512]
                        if ei < 0:
                            s.activation(dst, ps_z[zi % 2][:], AF.Gelu)
                            s_ze.inc(s.mul(dst, dst, 0.5))
                        else:
                            s.wait_ge(s_cw.h, 4 * (tb + 1))
                            s_ze.inc(s.activation(dst, ps_z[zi % 2][:], AF.Identity,
                                                  scale=c_sb[:, 4 * tb + sub, ei:ei + 1]))
                prev_blk = (ei, tb)
                widx += 1

    ctx.pop()
    PH2 = [(s_ze, ze_count[(EIS[-1], NTB - 1)]), (s_zm, zm_count[(EIS[-1], NTB - 1)])]

    ctx.push()
    fo = [ctx.sbuf("fo" + tag, [128, D], F32) for _ in range(2)]
    fx = [ctx.sbuf("fx" + tag, [128, D], F32) for _ in range(2)]
    yn = [ctx.sbuf("yn" + tag, [128, D], F32) for _ in range(2)]
    fw = [ctx.sbuf("fw" + tag, [128, D], F32) for _ in range(2)]
    d_fl = ctx.dstream("fl")
    s_fn = ctx.sem("fn")
    s_fv = ctx.sem("fv")
    d_xw = ctx.dstream("xw")
    ln3 = LNPipe(ctx, "fln" + tag)

    @block.sync
    def _(sync):
        for _s, _n in PH2:
            sync.wait_ge(_s.h, _n)
        d_ow.wait_all(sync)
        d_ul.wait_all(sync)
        for t in range(NTT + 2):
            if t < NTT:
                if t >= 2:
                    sync.wait_ge(s_fv.h, t - 1)
                d_fl.issue(sync.dma_start(out=fo[t % 2][:], in_=outacc_t[t]))
                d_fl.issue(sync.dma_start(out=fx[t % 2][:], in_=x_in_t[t]))
            if t >= 2:
                j = t - 2
                sync.wait_ge(s_fv.h, j + 1)
                d_xw.issue(sync.dma_start(out=x_out_t[j], in_=fw[j % 2][:]))
        d_xw.wait_all(sync)

    @block.scalar
    def _(s):
        for t in range(NTT + 1):
            if t < NTT:
                d_fl.wait_item(s, 2 * t)
                d_fl.wait_item(s, 2 * t + 1)
                ln3.sq(s, t, fo[t % 2][:])
            if t >= 1:
                j = t - 1
                if j >= 2:
                    s.wait_ge(s_fv.h, j - 1)
                ln3.norm(s, j, fo[j % 2][:], yn[j % 2][:], s_fn)

    @block.vector
    def _(v):
        for t in range(NTT + 1):
            if t < NTT:
                d_fl.wait_item(v, 2 * t)
                d_fl.wait_item(v, 2 * t + 1)
                ln3.stats(v, t, fo[t % 2][:])
            if t >= 1:
                j = t - 1
                pb = j % 2
                v.wait_ge(s_fn.h, j + 1)
                if j >= 2:
                    d_xw.wait_item(v, j - 2)
                v.tensor_tensor(yn[pb][:], yn[pb][:], gam_sb[:], ALU.mult)
                v.tensor_tensor(yn[pb][:], yn[pb][:], bet_sb[:], ALU.add)
                s_fv.inc(v.scalar_tensor_tensor(fw[pb][:], yn[pb][:], 0.5, fx[pb][:],
                                                ALU.mult, ALU.add))

    ctx.pop()
    return {"fv": (s_fv, NTT), "xw_stream": d_xw}




def causal_keep_tiles():
    r = np.arange(128)[:, None]
    c = np.arange(512)[None, :]
    return np.stack([((r + o) <= c).astype(np.float32) for o in (0, 128, 256, 384)])


def emit_attn1(ctx, io, tag):
    nc, block = ctx.nc, ctx.block
    NB = io.get("n_batches", 2)
    x_in_t = io["x_in"].rearrange("(b n p) d -> b n p d", p=128, b=NB)
    x_out_t = io["x_out"].rearrange("(b n p) d -> b n p d", p=128, b=NB)
    aout_dram = io["aout_scr"].rearrange("b (n p) d -> b n p d", p=128)

    cos_np, sin_np = rope_tables()
    cos_c = ctx.const("cos" + tag, np.ascontiguousarray(cos_np.reshape(NT, 128, 32).transpose(1, 0, 2)))
    sin_c = ctx.const("sin" + tag, np.ascontiguousarray(sin_np.reshape(NT, 128, 32).transpose(1, 0, 2)))
    keep_c = ctx.const("keep" + tag, np.ascontiguousarray(causal_keep_tiles().transpose(1, 0, 2)).astype(BF16NP))
    id_c = ctx.const("ida" + tag, IDENT_BF)
    onesfc_c = ctx.const("ofc" + tag, np.ones((128, 1), np.float32))
    onesfr_c = ctx.const("ofr" + tag, np.ones((1, 128), np.float32))
    onesbr_c = ctx.const("obr" + tag, np.ones((1, 128), np.float32).astype(BF16NP))

    ctx.push()
    cos_sb = ctx.sbuf("cosA" + tag, [128, NT, 32], F32)
    sin_sb = ctx.sbuf("sinA" + tag, [128, NT, 32], F32)
    keep_sb = ctx.sbuf("keepA" + tag, [128, 4, 512], BF16)
    id_sb = ctx.sbuf("idA" + tag, [128, 128], BF16)
    onesfc_sb = ctx.sbuf("ofcA" + tag, [128, 1], F32)
    onesfr_sb = ctx.sbuf("ofrA" + tag, [1, 128], F32)
    onesbr_sb = ctx.sbuf("obrA" + tag, [1, 128], BF16)
    wo_sb = ctx.sbuf("woA" + tag, [128, 8, 1024], BF16)
    bo_sb = ctx.sbuf("boA" + tag, [1, 1024], BF16)
    lam_sb = ctx.sbuf("lamA" + tag, [128, 16], F32)
    hnw_sb = ctx.sbuf("hnwA" + tag, [128, 8], F32)
    lscr = ctx.sbuf("lscrA" + tag, [64, 64], F32)
    lrow = ctx.sbuf("lrowA" + tag, [1, 32], F32)
    wq_sb = ctx.sbuf("wqA" + tag, [128, 8, 1024], BF16)
    wk_sb = ctx.sbuf("wkA" + tag, [128, 8, 1024], BF16)
    wv_sb = ctx.sbuf("wvA" + tag, [128, 8, 512], BF16)
    wb_sb = ctx.sbuf("wbA" + tag, [1, 3, 1024], BF16)
    q_fm = ctx.sbuf("qfmA" + tag, [128, 8, S], BF16)
    k_fm = ctx.sbuf("kfmA" + tag, [128, 8, S], BF16)
    vhat = ctx.sbuf("vhatA" + tag, [128, NT, 8 * 65], BF16)
    xt = [ctx.sbuf("xtA" + tag, [128, D], F32) for _ in range(2)]
    ub = [ctx.sbuf("ubA" + tag, [128, D], BF16) for _ in range(2)]
    uTs = [ctx.sbuf("uTsA" + tag, [128, 8, 128], BF16) for _ in range(2)]
    qro = [ctx.sbuf("qroA" + tag, [128, 1024], BF16) for _ in range(2)]
    kro = [ctx.sbuf("kroA" + tag, [128, 1024], BF16) for _ in range(2)]
    ptil = [ctx.sbuf("ptilA" + tag, [128, 512], BF16) for _ in range(4)]
    osc = ctx.sbuf("oscA" + tag, [128, 640], F32)
    ast = [ctx.sbuf("astA" + tag, [128, 4, 64], BF16) for _ in range(2)]
    a3t = [ctx.sbuf("a3tA" + tag, [128, 1024], BF16) for _ in range(2)]
    fw = [ctx.sbuf("fwA" + tag, [128, D], F32) for _ in range(2)]
    ln1 = LNPipe(ctx, "alnA" + tag)
    vch = ctx.chain("avcA" + tag)
    sch = ctx.chain("ascA" + tag)
    ps_tr = [ctx.psum("patr" + tag, [128, 128], BF16) for _ in range(2)]
    ps_p = [ctx.psum("pap" + tag, [128, 512], F32) for _ in range(2)]
    ps_s = [ctx.psum("pas" + tag, [128, 512], F32) for _ in range(2)]
    ps_o = [ctx.psum("pao" + tag, [128, 260], F32) for _ in range(2)]

    OSC = 640
    s_cst = ctx.sem("acst")
    s_lam = ctx.sem("alam")
    s_wo = ctx.sem("awo")
    d_x = ctx.dstream("axl")
    d_w = ctx.dstream("awl")
    d_ao = ctx.dstream("aao")
    d_a3 = ctx.dstream("aa3")
    d_xw = ctx.dstream("axw")
    s_u = ctx.sem("aub")
    s_tr = ctx.sem("atr")
    s_te = ctx.sem("ate")
    s_pm = ctx.sem("apm")
    s_ro = ctx.sem("aro")
    s_sx = ctx.sem("asx")
    s_px = ctx.sem("apx")
    s_mk = ctx.sem("amk")
    s_pv = ctx.sem("apv")
    s_cb = ctx.sem("acb")
    s_rs = ctx.sem("ars")
    s_av = ctx.sem("aav")
    s_ym = ctx.sem("aym")
    s_yv = ctx.sem("ayv")

    @block.gpsimd
    def _(gp):
        s_cst.inc(gp.dma_start(out=cos_sb[:], in_=cos_c.ap()), 16)
        s_cst.inc(gp.dma_start(out=sin_sb[:], in_=sin_c.ap()), 16)
        s_cst.inc(gp.dma_start(out=keep_sb[:], in_=keep_c.ap()), 16)
        s_cst.inc(gp.dma_start(out=id_sb[:], in_=id_c.ap()), 16)
        s_cst.inc(gp.dma_start(out=onesfc_sb[:], in_=onesfc_c.ap()), 16)
        s_cst.inc(gp.dma_start(out=onesfr_sb[:], in_=onesfr_c.ap()), 16)
        s_cst.inc(gp.dma_start(out=onesbr_sb[:], in_=onesbr_c.ap()), 16)
        s_cst.inc(gp.dma_start(out=bo_sb[:], in_=io["bo_row"]), 16)
        s_cst.inc(gp.dma_start(out=hnw_sb[:], in_=io["hnw_col"].rearrange("(c p) o -> p (c o)", p=128), allow_slow_non_contiguous=True), 16)
        for nm, col in (("lq1T", 0), ("lk1T", 1), ("lq2T", 2), ("lk2T", 3)):
            s_cst.inc(gp.dma_start(out=lscr[:, col * 16:(col + 1) * 16], in_=io[nm]), 16)
    CSTN = s_cst.n

    @block.vector
    def _(v):
        v.wait_ge(s_cst.h, CSTN)
        vch.edge(v, v.tensor_mul(lscr[:, 0:16], lscr[:, 0:16], lscr[:, 16:32]))
        vch.edge(v, v.tensor_mul(lscr[:, 32:48], lscr[:, 32:48], lscr[:, 48:64]))
        v.sem_inc(s_lam.h, 1)
    s_lam.n += 1

    @block.tensor
    def _(t):
        t.wait_ge(s_lam.h, 1)
        t.matmul(ps_o[0][0:1, 0:16], onesfc_sb[0:64, :], lscr[:, 0:16], start=True, stop=True)
        s_lam.inc(t.matmul(ps_o[1][0:1, 0:16], onesfc_sb[0:64, :], lscr[:, 32:48], start=True, stop=True))

    @block.scalar
    def _(s):
        s.wait_ge(s_lam.h, 2)
        s.activation(lrow[:, 0:16], ps_o[0][0:1, 0:16], AF.Exp)
        s_lam.inc(s.activation(lrow[:, 16:32], ps_o[1][0:1, 0:16], AF.Exp))

    @block.vector
    def _(v):
        v.wait_ge(s_lam.h, 3)
        vch.edge(v, v.tensor_sub(lrow[:, 0:16], lrow[:, 0:16], lrow[:, 16:32]))
        s_lam.inc(v.tensor_scalar_add(lrow[:, 0:16], lrow[:, 0:16], LAMBDA_INIT))

    @block.tensor
    def _(t):
        t.wait_ge(s_lam.h, 4)
        s_lam.inc(t.matmul(ps_o[0][:, 0:16], onesfr_sb[:], lrow[0:1, 0:16], start=True, stop=True))

    @block.vector
    def _(v):
        v.wait_ge(s_lam.h, 5)
        s_lam.inc(v.tensor_copy(lam_sb[:], ps_o[0][:, 0:16]))
    LAMN = s_lam.n

    d_wo = ctx.dstream("awo2")

    @block.sync
    def _(sync):
        WoTr = io["WoT"].rearrange("(c p) o -> c p o", p=128)
        for c in range(8):
            if c >= 2:
                sync.wait_ge(s_wo.h, c - 1)
            d_wo.issue(sync.dma_start(out=xt[c % 2][:], in_=WoTr[c]))

    @block.vector
    def _(v):
        v.wait_ge(s_cst.h, CSTN)
        for c in range(8):
            d_wo.wait_item(v, c)
            s_wo.inc(v.tensor_scalar(wo_sb[:, c, :], xt[c % 2][:], hnw_sb[:, c:c + 1],
                                     float(np.float32(1.0 - LAMBDA_INIT)), ALU.mult, ALU.mult))
    WON = s_wo.n

    kbs_of_qc = {qc: list(range(4 * qc + 4)) for qc in range(4)}
    KPH = sum(len(kbs_of_qc[qc]) for qc in range(4))
    end_markers = {}

    for b in range(NB):
        B_yv0 = s_yv.n
        for half in range(2):
            B_x, B_u = d_x.n_issued, s_u.n
            B_tr, B_te = s_tr.n, s_te.n
            B_pm, B_ro = s_pm.n, s_ro.n
            B_sx, B_px, B_mk, B_pv = s_sx.n, s_px.n, s_mk.n, s_pv.n
            B_cb, B_rs, B_av = s_cb.n, s_rs.n, s_av.n
            B_ao = d_ao.n_issued

            @block.gpsimd
            def _(gp, b=b, half=half, B_pm=B_pm):
                gp.wait_ge(s_wo.h, WON)
                gp.wait_ge(s_pm.h, B_pm)
                WqTr = io["WqTb"][0:1024, half * 1024:(half + 1) * 1024].rearrange("(c p) o -> c p o", p=128)
                WkTr = io["WkTb"][0:1024, half * 1024:(half + 1) * 1024].rearrange("(c p) o -> c p o", p=128)
                WvTr = io["WvTb"][0:1024, half * 512:(half + 1) * 512].rearrange("(c p) o -> c p o", p=128)
                for c in range(8):
                    d_w.issue(gp.dma_start(out=wq_sb[:, c, :], in_=WqTr[c]))
                for c in range(8):
                    d_w.issue(gp.dma_start(out=wk_sb[:, c, :], in_=WkTr[c]))
                for c in range(8):
                    d_w.issue(gp.dma_start(out=wv_sb[:, c, :], in_=WvTr[c]))
                d_w.issue(gp.dma_start(out=wb_sb[:, 0, :], in_=io["WqTb"][1024:1025, half * 1024:(half + 1) * 1024]))
                d_w.issue(gp.dma_start(out=wb_sb[:, 1, :], in_=io["WkTb"][1024:1025, half * 1024:(half + 1) * 1024]))
                d_w.issue(gp.dma_start(out=wb_sb[:, 2, 0:512], in_=io["WvTb"][1024:1025, half * 512:(half + 1) * 512]))

            @block.sync
            def _(sync, b=b, B_u=B_u, B_yv0=B_yv0):
                for i in range(NT):
                    if i >= 2:
                        sync.wait_ge(s_u.h, B_u + i - 1)
                    else:
                        sync.wait_ge(s_wo.h, WON)
                        sync.wait_ge(s_u.h, B_u)
                        sync.wait_ge(s_yv.h, B_yv0)
                    d_x.issue(sync.dma_start(out=xt[i % 2][:], in_=x_in_t[b, i]))

            lnb = 2 * b + half

            @block.scalar
            def _(s, B_x=B_x, B_tr=B_tr, lnb=lnb):
                for t in range(NT + 1):
                    if t < NT:
                        d_x.wait_item(s, B_x + t)
                        ln1.sq(s, lnb * NT + t, xt[t % 2][:])
                    if t >= 1:
                        j = t - 1
                        if j >= 2:
                            s.wait_ge(s_tr.h, B_tr + 24 * (j - 2) + 8)
                        ln1.norm(s, lnb * NT + j, xt[j % 2][:], ub[j % 2][:], s_u)

            @block.tensor
            def _(t, B_u=B_u, B_te=B_te, B_ro=B_ro):
                d_w.wait_all(t)
                for i in range(NT):
                    pb = i % 2
                    t.wait_ge(s_u.h, B_u + i + 1)
                    for c in range(8):
                        g = 24 * i + c
                        if g >= 2:
                            t.wait_ge(s_te.h, B_te + g - 1)
                        s_tr.inc(t.transpose(ps_tr[g % 2][:], ub[pb][:, c * 128:(c + 1) * 128], id_sb[:]))
                    t.wait_ge(s_te.h, B_te + 24 * i + 8)
                    for kind in range(5):
                        pi = 5 * i + kind
                        if pi >= 2:
                            t.wait_ge(s_ro.h, B_ro + pi - 1)
                        if kind < 2:
                            wsb, bcol, ncols, off = wq_sb, 0, 512, kind * 512
                        elif kind < 4:
                            wsb, bcol, ncols, off = wk_sb, 1, 512, (kind - 2) * 512
                        else:
                            wsb, bcol, ncols, off = wv_sb, 2, 512, 0
                        for c in range(8):
                            t.matmul(ps_p[pi % 2][:, 0:ncols], uTs[pb][:, c, :],
                                     wsb[:, c, off:off + ncols] if kind < 4 else wsb[:, c, :],
                                     start=(c == 0), stop=False)
                        s_pm.inc(t.matmul(ps_p[pi % 2][:, 0:ncols], onesbr_sb[:],
                                          wb_sb[:, bcol, off:off + ncols], start=False, stop=True))
                    t.wait_ge(s_ro.h, B_ro + 5 * (i + 1))
                    for c in range(16):
                        g = 24 * i + 8 + c
                        t.wait_ge(s_te.h, B_te + g - 1)
                        src = qro[pb] if c < 8 else kro[pb]
                        cc = c % 8
                        s_tr.inc(t.transpose(ps_tr[g % 2][:], src[:, cc * 128:(cc + 1) * 128], id_sb[:]))

            @block.vector
            def _(v, b=b, half=half, B_tr=B_tr, B_pm=B_pm, B_sx=B_sx, B_pv=B_pv, B_x=B_x):
                v.wait_ge(s_sx.h, B_sx)
                v.wait_ge(s_pv.h, B_pv)
                if half == 0:
                    v.memset(vhat[:].rearrange("p n (h e) -> p (n h) e", e=65)[:, :, 64:65], 1.0)
                lnb = 2 * b + half
                for i in range(NT):
                    pb = i % 2
                    d_x.wait_item(v, B_x + i)
                    ln1.stats(v, lnb * NT + i, xt[pb][:])
                    for c in range(8):
                        g = 24 * i + c
                        v.wait_ge(s_tr.h, B_tr + g + 1)
                        s_te.inc(v.tensor_copy(uTs[pb][:, c, :], ps_tr[g % 2][:]))
                    for kind in range(5):
                        pi = 5 * i + kind
                        v.wait_ge(s_pm.h, B_pm + pi + 1)
                        ps = ps_p[pi % 2]
                        if kind < 4:
                            dst = qro[pb] if kind < 2 else kro[pb]
                            off = (kind % 2) * 512
                            pv4 = ps[:, 0:512].rearrange("p (blk two c) -> p blk two c", two=2, c=32)
                            x1 = pv4[:, :, 0, :]
                            x2 = pv4[:, :, 1, :]
                            dv4 = dst[:, off:off + 512].rearrange("p (blk two c) -> p blk two c", two=2, c=32)
                            d1 = dv4[:, :, 0, :]
                            d2 = dv4[:, :, 1, :]
                            cos_ap = bass.AP(cos_sb, i * 32, [[NT * 32, 128], [0, 8], [1, 32]])
                            sin_ap = bass.AP(sin_sb, i * 32, [[NT * 32, 128], [0, 8], [1, 32]])
                            tmp = osc[:, 256:512].rearrange("p (blk c) -> p blk c", c=32)
                            v.tensor_tensor(d1, x1, cos_ap, ALU.mult)
                            v.tensor_tensor(tmp, x2, sin_ap, ALU.mult)
                            v.tensor_sub(d1, d1, tmp)
                            v.tensor_tensor(d2, x1, sin_ap, ALU.mult)
                            v.tensor_tensor(tmp, x2, cos_ap, ALU.mult)
                            s_ro.inc(v.tensor_add(d2, d2, tmp))
                        else:
                            dstv = vhat[:, i, :].rearrange("p (h e) -> p h e", e=65)[:, :, 0:64]
                            s_ro.inc(v.tensor_copy(dstv, ps[:, 0:512].rearrange("p (h e) -> p h e", e=64)))
                    for c in range(16):
                        g = 24 * i + 8 + c
                        v.wait_ge(s_tr.h, B_tr + g + 1)
                        dst = q_fm if c < 8 else k_fm
                        cc = c % 8
                        s_te.inc(v.tensor_copy(dst[:, cc, i * 128:(i + 1) * 128], ps_tr[g % 2][:]))

            P1_TE = B_te + 24 * NT
            P1_RO = B_ro + 5 * NT

            def kidx_of(hrel, qc, ki):
                return hrel * KPH + sum(len(kbs_of_qc[q]) for q in range(qc)) + ki

            def mkidx_of(hrel, qc, dj):
                return hrel * 16 + qc * 4 + dj

            @block.tensor
            def _(t, P1_TE=P1_TE, P1_RO=P1_RO, B_px=B_px, B_mk=B_mk, B_cb=B_cb):
                t.wait_ge(s_te.h, P1_TE)
                t.wait_ge(s_ro.h, P1_RO)
                for hrel in range(8):
                    for qc in range(4):
                        kbs = kbs_of_qc[qc]
                        cbidx = hrel * 4 + qc
                        if cbidx >= 1:
                            t.wait_ge(s_cb.h, B_cb + cbidx)
                        for ki, kb in enumerate(kbs):
                            kidx = kidx_of(hrel, qc, ki)
                            if kidx >= 2:
                                t.wait_ge(s_px.h, B_px + kidx - 1)
                            psA = ps_s[0] if kidx % 2 == 0 else ps_p[0]
                            psB = ps_s[1] if kidx % 2 == 0 else ps_p[1]
                            t.matmul(psA[:], k_fm[0:64, hrel, kb * 128:(kb + 1) * 128],
                                     q_fm[0:64, hrel, qc * 512:(qc + 1) * 512],
                                     start=True, stop=True, tile_position=(0, 0))
                            s_sx.inc(t.matmul(psB[:], k_fm[64:128, hrel, kb * 128:(kb + 1) * 128],
                                              q_fm[64:128, hrel, qc * 512:(qc + 1) * 512],
                                              start=True, stop=True, tile_position=(64, 0)))
                            if kb >= 4 * qc:
                                t.wait_ge(s_mk.h, B_mk + mkidx_of(hrel, qc, kb - 4 * qc) + 1)
                            else:
                                t.wait_ge(s_px.h, B_px + kidx + 1)
                            first, last = (ki == 0), (ki == len(kbs) - 1)
                            for grp in range(2):
                                for sub in range(4):
                                    mm = t.matmul(ps_o[grp][:, sub * 65:(sub + 1) * 65],
                                                  ptil[2 * (kidx % 2) + grp][:, sub * 128:(sub + 1) * 128],
                                                  vhat[:, kb, hrel * 65:(hrel + 1) * 65],
                                                  start=(first and sub == 0),
                                                  stop=(last and sub == 3))
                            s_pv.inc(mm)

            @block.scalar
            def _(s, B_sx=B_sx, B_pv=B_pv, B_cb=B_cb):
                for hrel in range(8):
                    for qc in range(4):
                        kbs = kbs_of_qc[qc]
                        cbidx = hrel * 4 + qc
                        for ki, kb in enumerate(kbs):
                            kidx = kidx_of(hrel, qc, ki)
                            s.wait_ge(s_sx.h, B_sx + kidx + 1)
                            if kidx >= 2:
                                s.wait_ge(s_pv.h, B_pv + kidx - 1)
                            psA = ps_s[0] if kidx % 2 == 0 else ps_p[0]
                            psB = ps_s[1] if kidx % 2 == 0 else ps_p[1]
                            s.activation(ptil[2 * (kidx % 2)][:], psA[:], AF.Exp, scale=0.125)
                            s_px.inc(s.activation(ptil[2 * (kidx % 2) + 1][:], psB[:], AF.Exp, scale=0.125))
                        s.wait_ge(s_cb.h, B_cb + cbidx + 1)
                        sch.edge(s, s.activation(osc[:, 552:556], osc[:, 532:536], AF.Ln))
                        s_rs.inc(s.activation(osc[:, 560:564], osc[:, 552:556], AF.Exp, scale=-0.5))

            @block.gpsimd
            def _(gp, B_px=B_px, B_pv=B_pv):
                for hrel in range(8):
                    for qc in range(4):
                        for ki, kb in enumerate(kbs_of_qc[qc]):
                            if kb < 4 * qc:
                                continue
                            kidx = kidx_of(hrel, qc, ki)
                            gp.wait_ge(s_px.h, B_px + kidx + 1)
                            doff = kb - 4 * qc
                            gp.tensor_mul(ptil[2 * (kidx % 2)][:], ptil[2 * (kidx % 2)][:],
                                          keep_sb[:, doff, :])
                            s_mk.inc(gp.tensor_mul(ptil[2 * (kidx % 2) + 1][:],
                                                   ptil[2 * (kidx % 2) + 1][:], keep_sb[:, doff, :]))

            @block.vector
            def _(v, b=b, half=half, B_pv=B_pv, B_rs=B_rs, B_ao=B_ao):
                for hrel in range(8):
                    habs = half * 8 + hrel
                    for qc in range(4):
                        cbidx = hrel * 4 + qc
                        kend = kidx_of(hrel, qc, len(kbs_of_qc[qc]) - 1) + 1
                        v.wait_ge(s_pv.h, B_pv + kend)
                        E = lambda ins: vch.edge(v, ins)
                        d1 = ps_o[0][:].rearrange("p (s e) -> p s e", e=65)[:, :, 64:65]
                        d2 = ps_o[1][:].rearrange("p (s e) -> p s e", e=65)[:, :, 64:65]
                        E(v.reciprocal(osc[:, 520:524].rearrange("p (s o) -> p s o", o=1), d1))
                        E(v.reciprocal(osc[:, 524:528].rearrange("p (s o) -> p s o", o=1), d2))
                        E(v.tensor_scalar_mul(osc[:, 524:528], osc[:, 524:528],
                                              lam_sb[:, habs:habs + 1]))
                        o1 = ps_o[0][:].rearrange("p (s e) -> p s e", e=65)[:, :, 0:64]
                        o2 = ps_o[1][:].rearrange("p (s e) -> p s e", e=65)[:, :, 0:64]
                        r1b = bass.AP(osc, 520, [[OSC, 128], [1, 4], [0, 64]])
                        r2b = bass.AP(osc, 524, [[OSC, 128], [1, 4], [0, 64]])
                        ov = osc[:, 0:256].rearrange("p (s e) -> p s e", e=64)
                        tv = osc[:, 256:512].rearrange("p (s e) -> p s e", e=64)
                        E(v.tensor_tensor(ov, o1, r1b, ALU.mult))
                        E(v.tensor_tensor(tv, o2, r2b, ALU.mult))
                        E(v.tensor_sub(osc[:, 0:256], osc[:, 0:256], osc[:, 256:512]))
                        E(v.tensor_mul(osc[:, 256:512], osc[:, 0:256], osc[:, 0:256]))
                        E(v.reduce_sum(osc[:, 528:532], tv, axis=AX.X))
                        s_cb.inc(v.tensor_scalar(osc[:, 532:536], osc[:, 528:532],
                                                 1.0 / 64, 1e-6, ALU.mult, ALU.add))
                        v.wait_ge(s_rs.h, B_rs + cbidx + 1)
                        avidx = cbidx
                        if avidx >= 2:
                            d_ao.wait_item(v, B_ao + avidx - 2)
                        rsb = bass.AP(osc, 560, [[OSC, 128], [1, 4], [0, 64]])
                        s_av.inc(v.tensor_tensor(ast[avidx % 2][:], ov, rsb, ALU.mult))

            @block.sync
            def _(sync, b=b, half=half, B_av=B_av):
                for hrel in range(8):
                    habs = half * 8 + hrel
                    for qc in range(4):
                        avidx = hrel * 4 + qc
                        sync.wait_ge(s_av.h, B_av + avidx + 1)
                        d_ao.issue(sync.dma_start(
                            out=aout_dram[b, qc * 4:(qc + 1) * 4, :, habs * 64:(habs + 1) * 64]
                                .rearrange("n p e -> p n e"),
                            in_=ast[avidx % 2][:]))

        B_tr3, B_te3 = s_tr.n, s_te.n
        B_u3 = s_u.n
        B_ym, B_yv = s_ym.n, s_yv.n
        B_x3 = d_x.n_issued
        B_a3 = d_a3.n_issued
        B_xw = d_xw.n_issued

        @block.sync
        def _(sync, b=b, B_yv=B_yv, B_a3=B_a3, B_x3=B_x3, B_xw=B_xw):
            d_ao.wait_all(sync)
            for t in range(NT + 2):
                if t < NT:
                    i = t
                    if i >= 2:
                        sync.wait_ge(s_ym.h, B_ym + i - 1)
                        sync.wait_ge(s_yv.h, B_yv + i - 1)
                    else:
                        sync.wait_ge(s_u.h, B_u3)
                    d_a3.issue(sync.dma_start(out=a3t[i % 2][:], in_=aout_dram[b, i]))
                    d_x.issue(sync.dma_start(out=xt[i % 2][:], in_=x_in_t[b, i]))
                if t >= 2:
                    j = t - 2
                    sync.wait_ge(s_yv.h, B_yv + j + 1)
                    d_xw.issue(sync.dma_start(out=x_out_t[b, j], in_=fw[j % 2][:]))
            if b == NB - 1:
                d_xw.wait_all(sync)

        @block.tensor
        def _(t, B_tr3=B_tr3, B_te3=B_te3, B_a3=B_a3, B_ym=B_ym, B_yv=B_yv):
            for i in range(NT):
                d_a3.wait_item(t, B_a3 + i)
                for c in range(8):
                    g = 8 * i + c
                    if g >= 2:
                        t.wait_ge(s_te.h, B_te3 + g - 1)
                    s_tr.inc(t.transpose(ps_tr[g % 2][:], a3t[i % 2][:, c * 128:(c + 1) * 128], id_sb[:]))
                t.wait_ge(s_te.h, B_te3 + 8 * (i + 1))
                if i >= 2:
                    t.wait_ge(s_yv.h, B_yv + i - 1)
                psA = ps_p[0] if i % 2 == 0 else ps_s[0]
                psB = ps_p[1] if i % 2 == 0 else ps_s[1]
                for half2, psX in ((0, psA), (1, psB)):
                    for c in range(8):
                        t.matmul(psX[:], uTs[i % 2][:, c, :],
                                 wo_sb[:, c, half2 * 512:(half2 + 1) * 512],
                                 start=(c == 0), stop=False)
                    mm = t.matmul(psX[:], onesbr_sb[:], bo_sb[:, half2 * 512:(half2 + 1) * 512],
                                  start=False, stop=True)
                s_ym.inc(mm)

        @block.vector
        def _(v, B_tr3=B_tr3, B_ym=B_ym, B_x3=B_x3, B_xw=B_xw):
            for i in range(NT):
                for c in range(8):
                    g = 8 * i + c
                    v.wait_ge(s_tr.h, B_tr3 + g + 1)
                    s_te.inc(v.tensor_copy(uTs[i % 2][:, c, :], ps_tr[g % 2][:]))
                v.wait_ge(s_ym.h, B_ym + i + 1)
                d_x.wait_item(v, B_x3 + i)
                if i >= 2:
                    d_xw.wait_item(v, B_xw + i - 2)
                psA = ps_p[0] if i % 2 == 0 else ps_s[0]
                psB = ps_p[1] if i % 2 == 0 else ps_s[1]
                v.tensor_add(fw[i % 2][:, 0:512], psA[:], xt[i % 2][:, 0:512])
                s_yv.inc(v.tensor_add(fw[i % 2][:, 512:1024], psB[:], xt[i % 2][:, 512:1024]))

    ctx.pop()
    return {"tensor": (s_ym, s_ym.n), "vector": (s_yv, s_yv.n),
            "scalar": (s_rs, s_rs.n), "gpsimd": (s_mk, s_mk.n),
            "xw": d_xw}




def emit_attn0(ctx, io, tag):
    nc, block = ctx.nc, ctx.block
    NB = io.get("n_batches", 2)
    x_in_t = io["x_in"].rearrange("(b n p) d -> b n p d", p=128, b=NB)
    x_out_t = io["x_out"].rearrange("(b n p) d -> b n p d", p=128, b=NB)

    id_c = ctx.const("id0" + tag, IDENT_BF)
    onesbr_c = ctx.const("obr0" + tag, np.ones((1, 128), np.float32).astype(BF16NP))

    ctx.push()
    id_sb = ctx.sbuf("id0" + tag, [128, 128], BF16)
    onesbr_sb = ctx.sbuf("obr0" + tag, [1, 128], BF16)
    wqkv_sb = ctx.sbuf("wqkv0" + tag, [128, 8, 3072], BF16)
    wp_sb = ctx.sbuf("wp0" + tag, [128, 8, 1024], BF16)
    bp_sb = ctx.sbuf("bp0" + tag, [1, 1024], BF16)
    khat = ctx.sbuf("khat0" + tag, [128, NT, 1024], BF16)
    vaug = ctx.sbuf("vaug0" + tag, [128, NT, 16 * 65], BF16)
    qfm = ctx.sbuf("qfm0" + tag, [128, 8, S], BF16)
    kvs = ctx.sbuf("kvs0" + tag, [128, 8, 65], BF16)
    kvq = ctx.sbuf("kvq0" + tag, [64, 8, 65], BF16)
    xt = [ctx.sbuf("xt0" + tag, [128, D], F32) for _ in range(2)]
    ub = [ctx.sbuf("ub0" + tag, [128, D], BF16) for _ in range(2)]
    uTs = [ctx.sbuf("uTs0" + tag, [128, 8, 128], BF16) for _ in range(2)]
    fm1 = [ctx.sbuf("fm10" + tag, [128, 512], F32) for _ in range(2)]
    fm2 = [ctx.sbuf("fm20" + tag, [128, 512], F32) for _ in range(2)]
    aot = [ctx.sbuf("aot0" + tag, [128, 1024], BF16) for _ in range(2)]
    osc = ctx.sbuf("osc0" + tag, [128, 32], F32)
    ln1 = LNPipe(ctx, "a0ln" + tag)
    vch = ctx.chain("a0vc" + tag)
    ps_tr = [ctx.psum("p0tr" + tag, [128, 128], BF16) for _ in range(2)]
    ps_p = [ctx.psum("p0p" + tag, [128, 512], F32) for _ in range(2)]
    ps_o2 = [ctx.psum("p0o" + tag, [128, 260], F32) for _ in range(4)]
    ps_o = ps_o2[0:2]

    s_cst = ctx.sem("0cst")
    d_x = ctx.dstream("0xl")
    d_w = ctx.dstream("0wl")
    d_xw = ctx.dstream("0xw")
    s_u = ctx.sem("0ub")
    s_tr = ctx.sem("0tr")
    s_te = ctx.sem("0te")
    s_pm = ctx.sem("0pm")
    s_t1 = ctx.sem("0t1")
    s_t2 = ctx.sem("0t2")
    s_pr = ctx.sem("0pr")
    s_kv = ctx.sem("0kv")
    s_ke = ctx.sem("0ke")
    s_om = ctx.sem("0om")
    s_zf = ctx.sem("0zf")
    s_ym = ctx.sem("0ym")
    s_yv = ctx.sem("0yv")

    @block.gpsimd
    def _(gp):
        s_cst.inc(gp.dma_start(out=id_sb[:], in_=id_c.ap()), 16)
        s_cst.inc(gp.dma_start(out=onesbr_sb[:], in_=onesbr_c.ap()), 16)
        Wr = io["WqkvT"].rearrange("(c p) o -> c p o", p=128)
        for c in range(8):
            s_cst.inc(gp.dma_start(out=wqkv_sb[:, c, :], in_=Wr[c]), 16)
        Pr = io["WprojTb"][0:1024, :].rearrange("(c p) o -> c p o", p=128)
        for c in range(8):
            s_cst.inc(gp.dma_start(out=wp_sb[:, c, :], in_=Pr[c]), 16)
        s_cst.inc(gp.dma_start(out=bp_sb[:], in_=io["WprojTb"][1024:1025, :]), 16)
    CSTN = s_cst.n

    for b in range(NB):
        B_x, B_u = d_x.n_issued, s_u.n
        B_tr, B_te = s_tr.n, s_te.n
        B_pm, B_t1, B_t2, B_pr = s_pm.n, s_t1.n, s_t2.n, s_pr.n
        B_kv, B_ke = s_kv.n, s_ke.n
        B_om, B_zf, B_ym, B_yv = s_om.n, s_zf.n, s_ym.n, s_yv.n
        B_xw = d_xw.n_issued
        lnb = b

        @block.sync
        def _(sync, b=b, B_u=B_u, B_yv=B_yv):
            for i in range(NT):
                if i >= 2:
                    sync.wait_ge(s_u.h, B_u + i - 1)
                else:
                    sync.wait_ge(s_cst.h, CSTN)
                    sync.wait_ge(s_u.h, B_u)
                    sync.wait_ge(s_yv.h, B_yv)
                d_x.issue(sync.dma_start(out=xt[i % 2][:], in_=x_in_t[b, i]))

        @block.scalar
        def _(s, B_x=B_x, B_tr=B_tr, B_t1=B_t1, lnb=lnb):
            for t in range(NT + 1):
                if t < NT:
                    d_x.wait_item(s, B_x + t)
                    ln1.sq(s, lnb * NT + t, xt[t % 2][:])
                if t >= 1:
                    j = t - 1
                    if j >= 2:
                        s.wait_ge(s_tr.h, B_tr + 16 * (j - 1))
                    ln1.norm(s, lnb * NT + j, xt[j % 2][:], ub[j % 2][:], s_u)
                    for cch in range(4):
                        fi = 4 * j + cch
                        s.wait_ge(s_t1.h, B_t1 + fi + 1)
                        s_t2.inc(s.activation(fm2[fi % 2][:], fm1[fi % 2][:], AF.Exp))

        @block.tensor
        def _(t, b=b, B_u=B_u, B_te=B_te, B_pr=B_pr):
            t.wait_ge(s_cst.h, CSTN)
            for i in range(NT):
                pb = i % 2
                t.wait_ge(s_u.h, B_u + i + 1)
                for c in range(8):
                    g = 16 * i + c
                    if g >= 2:
                        t.wait_ge(s_te.h, B_te + g - 1)
                    s_tr.inc(t.transpose(ps_tr[g % 2][:], ub[pb][:, c * 128:(c + 1) * 128], id_sb[:]))
                t.wait_ge(s_te.h, B_te + 16 * i + 8)
                for ch in range(6):
                    pi = 6 * i + ch
                    if pi >= 2:
                        t.wait_ge(s_pr.h, B_pr + pi - 1)
                    for c in range(8):
                        mm = t.matmul(ps_p[pi % 2][:], uTs[pb][:, c, :],
                                      wqkv_sb[:, c, ch * 512:(ch + 1) * 512],
                                      start=(c == 0), stop=(c == 7))
                    s_pm.inc(mm)
                t.wait_ge(s_pr.h, B_pr + 6 * i + 4)
                for c in range(8):
                    g = 16 * i + 8 + c
                    t.wait_ge(s_te.h, B_te + g - 1)
                    s_tr.inc(t.transpose(ps_tr[g % 2][:], ub[pb][:, c * 128:(c + 1) * 128], id_sb[:]))

        @block.vector
        def _(v, b=b, B_x=B_x, B_tr=B_tr, B_pm=B_pm, B_t2=B_t2, B_om=B_om, lnb=lnb):
            if b > 0:
                v.wait_ge(s_om.h, B_om)
            v.memset(vaug[:].rearrange("p n (h e) -> p (n h) e", e=65)[:, :, 64:65], 1.0)
            for i in range(NT):
                pb = i % 2
                d_x.wait_item(v, B_x + i)
                ln1.stats(v, lnb * NT + i, xt[pb][:])
                for c in range(8):
                    g = 16 * i + c
                    v.wait_ge(s_tr.h, B_tr + g + 1)
                    s_te.inc(v.tensor_copy(uTs[pb][:, c, :], ps_tr[g % 2][:]))
                for ch in range(6):
                    pi = 6 * i + ch
                    v.wait_ge(s_pm.h, B_pm + pi + 1)
                    ps = ps_p[pi % 2]
                    if ch < 4:
                        fi = 4 * i + ch
                        s_t1.inc(v.tensor_scalar_min(fm1[fi % 2][:], ps[:], 0.0))
                        v.wait_ge(s_t2.h, B_t2 + fi + 1)
                        if ch < 2:
                            dst = ub[pb][:, ch * 512:(ch + 1) * 512]
                        else:
                            dst = khat[:, i, (ch - 2) * 512:(ch - 1) * 512]
                        s_pr.inc(v.scalar_tensor_tensor(dst, ps[:], 0.0, fm2[fi % 2][:],
                                                        ALU.max, ALU.add))
                    else:
                        hb = (ch - 4) * 8
                        dstv = vaug[:, i, hb * 65:(hb + 8) * 65].rearrange("p (h e) -> p h e", e=65)[:, :, 0:64]
                        s_pr.inc(v.tensor_copy(dstv, ps[:].rearrange("p (h e) -> p h e", e=64)))
                for c in range(8):
                    g = 16 * i + 8 + c
                    v.wait_ge(s_tr.h, B_tr + g + 1)
                    s_te.inc(v.tensor_copy(qfm[:, c, i * 128:(i + 1) * 128], ps_tr[g % 2][:]))

        if b == 0 and "dbg_khat" in io:
            s_dbg = ctx.sem("0dbg")

            @block.sync
            def _(sync, B_pr=B_pr, B_te=B_te):
                sync.wait_ge(s_pr.h, B_pr + 6 * NT)
                sync.wait_ge(s_te.h, B_te + 16 * NT)
                s_dbg.inc(sync.dma_start(out=io["dbg_khat"], in_=khat[:]), 16)
                s_dbg.inc(sync.dma_start(out=io["dbg_vaug"], in_=vaug[:]), 16)
                s_dbg.inc(sync.dma_start(out=io["dbg_qfm"], in_=qfm[:]), 16)
                sync.wait_ge(s_dbg.h, 48)

        if A0_STOP == "passA":
            @block.sync
            def _(sync, b=b):
                sync.dma_start(out=io["x_out"], in_=io["x_in"]).then_inc(s_cst.h, 16)
                sync.wait_ge(s_cst.h, CSTN + 16 * (b + 1))
            continue

        PA_PR = B_pr + 6 * NT

        @block.tensor
        def _(t, PA_PR=PA_PR, B_ke=B_ke, B_zf=B_zf):
            t.wait_ge(s_pr.h, PA_PR)
            t.wait_ge(s_zf.h, B_zf)
            for h in range(16):
                if h >= 4:
                    t.wait_ge(s_ke.h, B_ke + h - 3)
                bank = ps_o2[h % 4]
                nmm = A0_NMM if A0_STOP == "passBnomm" else NT
                for i in range(nmm):
                    mm = t.matmul(bank[0:64, 0:65], khat[:, i, h * 64:(h + 1) * 64],
                                  vaug[:, i, h * 65:(h + 1) * 65],
                                  start=(i == 0), stop=(i == nmm - 1))
                s_kv.inc(mm)

        @block.scalar
        def _(s, B_kv=B_kv):
            for h in range(16):
                s.wait_ge(s_kv.h, B_kv + h + 1)
                bank = ps_o2[h % 4]
                if h % 2 == 0:
                    s_ke.inc(s.activation(kvs[0:64, h // 2, :], bank[0:64, 0:65], AF.Identity))
                else:
                    s_ke.inc(s.activation(kvq[:, h // 2, :], bank[0:64, 0:65], AF.Identity))

        d_kv = ctx.dstream("0kvb" + str(b))

        @block.gpsimd
        def _(gp, B_ke=B_ke):
            gp.wait_ge(s_ke.h, B_ke + 16)
            if A0_STOP == "nobounce":
                gp.sem_inc(d_kv.s[0].h, 16)
                gp.sem_inc(d_kv.s[1].h, 16)
                d_kv.n_issued += 2
                return
            d_kv.issue(gp.dma_start(out=io["kv_scr"], in_=kvq[:]))
            d_kv.wait_item(gp, 0)
            d_kv.issue(gp.dma_start(out=kvs[64:128, :, :], in_=io["kv_scr"]))

        if b == 0 and "dbg_kvs" in io:
            s_dbg2 = ctx.sem("0dbg2")

            @block.sync
            def _(sync, B_ke=B_ke):
                sync.wait_ge(s_ke.h, B_ke + 16)
                d_kv.wait_item(sync, 1)
                s_dbg2.inc(sync.dma_start(out=io["dbg_kvs"], in_=kvs[:]), 16)
                sync.wait_ge(s_dbg2.h, 16)

        if A0_STOP in ("passB", "passBnomm"):
            @block.sync
            def _(sync, b=b, B_ke=B_ke):
                sync.wait_ge(s_ke.h, B_ke + 16)
                d_kv.wait_item(sync, 1)
                sync.dma_start(out=io["x_out"], in_=io["x_in"]).then_inc(s_cst.h, 16)
                sync.wait_ge(s_cst.h, CSTN + 16 * (b + 1))
            continue

        PB_KE = B_ke + 16
        B3_tr, B3_te = s_tr.n, s_te.n
        B_x2 = d_x.n_issued

        @block.sync
        def _(sync, b=b, B_yv=B_yv, B_u=B_u, B_xw=B_xw):
            for t in range(NT + 2):
                if t >= 2:
                    j = t - 2
                    sync.wait_ge(s_yv.h, B_yv + j + 1)
                    d_xw.issue(sync.dma_start(out=x_out_t[b, j], in_=xt[j % 2][:]))
                if t < NT:
                    i = t
                    if i >= 2:
                        d_xw.wait_item(sync, B_xw + i - 2)
                    else:
                        sync.wait_ge(s_u.h, B_u + NT)
                    d_x.issue(sync.dma_start(out=xt[i % 2][:], in_=x_in_t[b, i]))
            if b == NB - 1:
                d_xw.wait_all(sync)

        @block.tensor
        def _(t, PB_KE=PB_KE, B_zf=B_zf, B3_tr=B3_tr, B3_te=B3_te, B_yv=B_yv, B_ym=B_ym):
            t.wait_ge(s_ke.h, PB_KE)
            d_kv.wait_item(t, 1)
            for i in range(NT):
                if i >= 1:
                    t.wait_ge(s_zf.h, B_zf + i)
                seen = [0, 0, 0, 0]
                for h in range(16):
                    rows0 = (h % 2 == 0)
                    bk = 2 * (h % 2) + (h // 8)
                    col = ((h // 2) % 4) * 65
                    seen[bk] += 1
                    mm = t.matmul(ps_o2[bk][:, col:col + 65],
                                  qfm[0:64, h // 2, i * 128:(i + 1) * 128] if rows0 else
                                  qfm[64:128, h // 2, i * 128:(i + 1) * 128],
                                  kvs[0:64, h // 2, :] if rows0 else kvs[64:128, h // 2, :],
                                  start=(seen[bk] == 1), stop=(seen[bk] == 4),
                                  tile_position=(0, 0) if rows0 else (64, 0),
                                  skip_group_check=True)
                s_om.inc(mm)
                t.wait_ge(s_zf.h, B_zf + i + 1)
                for c in range(8):
                    g = 8 * i + c
                    if g >= 2:
                        t.wait_ge(s_te.h, B3_te + g - 1)
                    s_tr.inc(t.transpose(ps_tr[g % 2][:], aot[i % 2][:, c * 128:(c + 1) * 128], id_sb[:]))
                t.wait_ge(s_te.h, B3_te + 8 * (i + 1))
                if i >= 2:
                    t.wait_ge(s_yv.h, B_yv + i - 1)
                for half2 in range(2):
                    psX = ps_p[half2]
                    for c in range(8):
                        t.matmul(psX[:], uTs[i % 2][:, c, :],
                                 wp_sb[:, c, half2 * 512:(half2 + 1) * 512],
                                 start=(c == 0), stop=False)
                    mm = t.matmul(psX[:], onesbr_sb[:], bp_sb[:, half2 * 512:(half2 + 1) * 512],
                                  start=False, stop=True)
                s_ym.inc(mm)

        @block.vector
        def _(v, B_om=B_om, B_ym=B_ym, B_x2=B_x2, B_xw=B_xw, B3_tr=B3_tr, B3_te=B3_te, B_ke=B_ke):
            for i in range(NT):
                v.wait_ge(s_om.h, B_om + i + 1)
                if i >= 2:
                    v.wait_ge(s_tr.h, B3_tr + 8 * (i - 1))
                E = lambda ins: vch.edge(v, ins)
                for bk in range(4):
                    dv = ps_o2[bk][:].rearrange("p (h e) -> p h e", e=65)[:, :, 64:65]
                    E(v.tensor_scalar_add(osc[:, bk * 4:bk * 4 + 4].rearrange("p (h o) -> p h o", o=1), dv, 1e-6))
                    E(v.reciprocal(osc[:, 16 + bk * 4:16 + bk * 4 + 4], osc[:, bk * 4:bk * 4 + 4]))
                last = None
                for bk in range(4):
                    ov = ps_o2[bk][:].rearrange("p (h e) -> p h e", e=65)[:, :, 0:64]
                    rb = bass.AP(osc, 16 + bk * 4, [[32, 128], [1, 4], [0, 64]])
                    h0 = 2 * (bk % 2) * 0 + ((bk % 2) if False else 0)
                    h0 = (bk // 2) + 8 * 0 if False else None
                    hstart = (1 if bk >= 2 else 0) + 8 * (bk % 2)
                    dst = bass.AP(aot[i % 2], hstart * 64, [[1024, 128], [128, 4], [1, 64]])
                    last = v.tensor_tensor(dst, ov, rb, ALU.mult)
                s_zf.inc(last)
                for c in range(8):
                    g = 8 * i + c
                    v.wait_ge(s_tr.h, B3_tr + g + 1)
                    s_te.inc(v.tensor_copy(uTs[i % 2][:, c, :], ps_tr[g % 2][:]))
                v.wait_ge(s_ym.h, B_ym + i + 1)
                d_x.wait_item(v, B_x2 + i)
                v.tensor_add(xt[i % 2][:, 0:512], ps_p[0][:], xt[i % 2][:, 0:512])
                s_yv.inc(v.tensor_add(xt[i % 2][:, 512:1024], ps_p[1][:], xt[i % 2][:, 512:1024]))

    ctx.pop()
    return {}



def emit_ffn0a(ctx, io, tag):
    nc, block = ctx.nc, ctx.block
    NB = io.get("n_batches", 2)
    x_in_t = io["x_in"].rearrange("(b n p) d -> b n p d", p=128, b=NB)
    probs_t = io["probs_out"].rearrange("b (n p) h -> b n p h", p=128)
    qk_t = io["qk_out"].rearrange("b (n p) h -> b n p h", p=128)
    v_t = io["v_out"].rearrange("b (n p) d -> b n p d", p=128)

    id_c = ctx.const("idf" + tag, IDENT_BF)
    onesbr_c = ctx.const("obrf" + tag, np.ones((1, 128), np.float32).astype(BF16NP))

    ctx.push()
    id_sb = ctx.sbuf("idf" + tag, [128, 128], BF16)
    onesbr_sb = ctx.sbuf("obrf" + tag, [1, 128], BF16)
    wg_sb = ctx.sbuf("wgf" + tag, [128, 8, 16], BF16)
    wq_sb = ctx.sbuf("wqf" + tag, [128, 8, 1024], BF16)
    wkv_sb = ctx.sbuf("wkvf" + tag, [128, 8, 2048], BF16)
    wb_sb = ctx.sbuf("wbf" + tag, [1, 3072], BF16)
    xt = [ctx.sbuf("xtf" + tag, [128, D], F32) for _ in range(2)]
    ub = [ctx.sbuf("ubf" + tag, [128, D], BF16) for _ in range(2)]
    uTs = [ctx.sbuf("uTsf" + tag, [128, 8, 128], BF16) for _ in range(2)]
    qsb = ctx.sbuf("qsbf" + tag, [128, 1024], F32)
    qke = ctx.sbuf("qkef" + tag, [128, 1024], F32)
    gp_sb = [ctx.sbuf("gpf" + tag, [128, 16], F32) for _ in range(2)]
    qk_sb = [ctx.sbuf("qkf" + tag, [128, 16], F32) for _ in range(2)]
    vv_sb = [ctx.sbuf("vvf" + tag, [128, 1024], BF16) for _ in range(2)]
    gscr = ctx.sbuf("gscrf" + tag, [128, 8], F32)
    ln1 = LNPipe(ctx, "f0ln" + tag)
    vch = ctx.chain("f0vc" + tag)
    ps_tr = [ctx.psum("pftr" + tag, [128, 128], BF16) for _ in range(2)]
    ps_q = [ctx.psum("pfq" + tag, [128, 512], F32) for _ in range(2)]
    ps_k = [ctx.psum("pfk" + tag, [128, 512], F32) for _ in range(2)]
    ps_g = ctx.psum("pfg" + tag, [128, 16], F32)

    s_cst = ctx.sem("fcst")
    d_x = ctx.dstream("fxl")
    d_po = ctx.dstream("fpo")
    d_qo = ctx.dstream("fqo")
    d_vo = ctx.dstream("fvo")
    s_u = ctx.sem("fub")
    s_tr = ctx.sem("ftr")
    s_te = ctx.sem("fte")
    s_gg = ctx.sem("fgg")
    s_gmx = ctx.sem("fgmx")
    s_gx = ctx.sem("fgx")
    s_gv = ctx.sem("fgv")
    s_qm = ctx.sem("fqm")
    s_qe = ctx.sem("fqe")
    s_km = ctx.sem("fkm")
    s_kd = ctx.sem("fkd")
    s_vm = ctx.sem("fvm")
    s_ve = ctx.sem("fve")

    @block.gpsimd
    def _(gp):
        s_cst.inc(gp.dma_start(out=id_sb[:], in_=id_c.ap()), 16)
        s_cst.inc(gp.dma_start(out=onesbr_sb[:], in_=onesbr_c.ap()), 16)
        s_cst.inc(gp.dma_start(out=wg_sb[:], in_=io["Wg"].rearrange("(c p) h -> p c h", p=128)), 16)
        Wqr = io["WqcTb"][0:1024, :].rearrange("(c p) o -> c p o", p=128)
        for c in range(8):
            s_cst.inc(gp.dma_start(out=wq_sb[:, c, :], in_=Wqr[c]), 16)
        Wkr = io["WkvcTb"][0:1024, :].rearrange("(c p) o -> c p o", p=128)
        for c in range(8):
            s_cst.inc(gp.dma_start(out=wkv_sb[:, c, :], in_=Wkr[c]), 16)
        s_cst.inc(gp.dma_start(out=wb_sb[:, 0:1024], in_=io["WqcTb"][1024:1025, :]), 16)
        s_cst.inc(gp.dma_start(out=wb_sb[:, 1024:3072], in_=io["WkvcTb"][1024:1025, :]), 16)
    CSTN = s_cst.n

    for b in range(NB):
        B_x, B_u = d_x.n_issued, s_u.n
        B_tr, B_te = s_tr.n, s_te.n
        B_gg, B_gx, B_gv, B_gmx = s_gg.n, s_gx.n, s_gv.n, s_gmx.n
        B_qm, B_qe, B_km, B_kd = s_qm.n, s_qe.n, s_km.n, s_kd.n
        B_vm, B_ve = s_vm.n, s_ve.n
        B_po, B_qo, B_vo = d_po.n_issued, d_qo.n_issued, d_vo.n_issued
        lnb = b

        @block.sync
        def _(sync, b=b, B_u=B_u, B_gv=B_gv, B_kd=B_kd, B_ve=B_ve):
            for t in range(NT + 1):
                if t < NT:
                    i = t
                    if i >= 2:
                        sync.wait_ge(s_u.h, B_u + i - 1)
                    else:
                        sync.wait_ge(s_cst.h, CSTN)
                        sync.wait_ge(s_u.h, B_u)
                    d_x.issue(sync.dma_start(out=xt[i % 2][:], in_=x_in_t[b, i]))
                if t >= 1:
                    j = t - 1
                    sync.wait_ge(s_gv.h, B_gv + j + 1)
                    d_po.issue(sync.dma_start(out=probs_t[b, j], in_=gp_sb[j % 2][:]))
                    sync.wait_ge(s_kd.h, B_kd + j + 1)
                    d_qo.issue(sync.dma_start(out=qk_t[b, j], in_=qk_sb[j % 2][:]))
                    sync.wait_ge(s_ve.h, B_ve + j + 1)
                    d_vo.issue(sync.dma_start(out=v_t[b, j], in_=vv_sb[j % 2][:]))
            if b == NB - 1:
                d_po.wait_all(sync)
                d_qo.wait_all(sync)
                d_vo.wait_all(sync)

        @block.scalar
        def _(s, B_x=B_x, B_tr=B_tr, B_gmx=B_gmx, B_qm=B_qm, B_po=B_po, lnb=lnb):
            for t in range(NT + 1):
                if t < NT:
                    d_x.wait_item(s, B_x + t)
                    ln1.sq(s, lnb * NT + t, xt[t % 2][:])
                if t >= 1:
                    j = t - 1
                    if j >= 2:
                        s.wait_ge(s_tr.h, B_tr + 8 * (j - 1))
                    ln1.norm(s, lnb * NT + j, xt[j % 2][:], ub[j % 2][:], s_u)
                    s.wait_ge(s_gmx.h, B_gmx + j + 1)
                    if j >= 2:
                        d_po.wait_item(s, B_po + j - 2)
                    s_gx.inc(s.activation(gp_sb[j % 2][:], ps_g[:], AF.Exp,
                                          bias=gscr[:, (j % 2) + 2:(j % 2) + 3]))
                    s.wait_ge(s_qm.h, B_qm + 2 * (j + 1))
                    s.activation(qsb[:, 0:512], ps_q[0][:], AF.Identity)
                    s_qe.inc(s.activation(qsb[:, 512:1024], ps_q[1][:], AF.Identity))

        @block.tensor
        def _(t, B_u=B_u, B_te=B_te, B_gx=B_gx, B_qe=B_qe, B_kd=B_kd, B_ve=B_ve, B_gv=B_gv):
            t.wait_ge(s_cst.h, CSTN)
            for i in range(NT):
                pb = i % 2
                t.wait_ge(s_u.h, B_u + i + 1)
                for c in range(8):
                    g = 8 * i + c
                    if g >= 2:
                        t.wait_ge(s_te.h, B_te + g - 1)
                    s_tr.inc(t.transpose(ps_tr[g % 2][:], ub[pb][:, c * 128:(c + 1) * 128], id_sb[:]))
                t.wait_ge(s_te.h, B_te + 8 * (i + 1))
                if i >= 1:
                    t.wait_ge(s_gx.h, B_gx + i)
                for c in range(8):
                    mm = t.matmul(ps_g[:], uTs[pb][:, c, :], wg_sb[:, c, :],
                                  start=(c == 0), stop=(c == 7))
                s_gg.inc(mm)
                if i >= 1:
                    t.wait_ge(s_qe.h, B_qe + i)
                for half in range(2):
                    for c in range(8):
                        t.matmul(ps_q[half][:], uTs[pb][:, c, :],
                                 wq_sb[:, c, half * 512:(half + 1) * 512],
                                 start=(c == 0), stop=False)
                    s_qm.inc(t.matmul(ps_q[half][:], onesbr_sb[:],
                                      wb_sb[:, half * 512:(half + 1) * 512],
                                      start=False, stop=True))
                if i >= 1:
                    t.wait_ge(s_kd.h, B_kd + i)
                for half in range(2):
                    for c in range(8):
                        t.matmul(ps_k[half][:], uTs[pb][:, c, :],
                                 wkv_sb[:, c, half * 512:(half + 1) * 512],
                                 start=(c == 0), stop=False)
                    s_km.inc(t.matmul(ps_k[half][:], onesbr_sb[:],
                                      wb_sb[:, 1024 + half * 512:1024 + (half + 1) * 512],
                                      start=False, stop=True))
                t.wait_ge(s_qe.h, B_qe + i + 1)
                if i >= 1:
                    t.wait_ge(s_ve.h, B_ve + i)
                for half in range(2):
                    for c in range(8):
                        t.matmul(ps_q[half][:], uTs[pb][:, c, :],
                                 wkv_sb[:, c, 1024 + half * 512:1024 + (half + 1) * 512],
                                 start=(c == 0), stop=False)
                    s_vm.inc(t.matmul(ps_q[half][:], onesbr_sb[:],
                                      wb_sb[:, 2048 + half * 512:2048 + (half + 1) * 512],
                                      start=False, stop=True))

        @block.vector
        def _(v, B_x=B_x, B_tr=B_tr, B_gg=B_gg, B_gx=B_gx, B_km=B_km, B_qe=B_qe, B_vm=B_vm, B_qo=B_qo, B_vo=B_vo, lnb=lnb):
            for i in range(NT):
                pb = i % 2
                d_x.wait_item(v, B_x + i)
                ln1.stats(v, lnb * NT + i, xt[pb][:])
                for c in range(8):
                    g = 8 * i + c
                    v.wait_ge(s_tr.h, B_tr + g + 1)
                    s_te.inc(v.tensor_copy(uTs[pb][:, c, :], ps_tr[g % 2][:]))
                v.wait_ge(s_gg.h, B_gg + i + 1)
                vch.edge(v, v.reduce_max(gscr[:, pb:pb + 1], ps_g[:], axis=AX.X))
                s_gmx.inc(v.tensor_scalar_mul(gscr[:, pb + 2:pb + 3], gscr[:, pb:pb + 1], -1.0))
                v.wait_ge(s_gx.h, B_gx + i + 1)
                vch.edge(v, v.reduce_sum(gscr[:, 4:5], gp_sb[pb][:], axis=AX.X))
                vch.edge(v, v.reciprocal(gscr[:, 5:6], gscr[:, 4:5]))
                s_gv.inc(v.tensor_scalar_mul(gp_sb[pb][:], gp_sb[pb][:], gscr[:, 5:6]))
                v.wait_ge(s_km.h, B_km + 2 * (i + 1))
                v.wait_ge(s_qe.h, B_qe + i + 1)
                if i >= 2:
                    d_qo.wait_item(v, B_qo + i - 2)
                v.tensor_tensor(qke[:, 0:512], qsb[:, 0:512], ps_k[0][:], ALU.mult)
                v.tensor_tensor(qke[:, 512:1024], qsb[:, 512:1024], ps_k[1][:], ALU.mult)
                s_kd.inc(v.reduce_sum(qk_sb[i % 2][:], qke[:].rearrange("p (h e) -> p h e", e=64), axis=AX.X))
                v.wait_ge(s_vm.h, B_vm + 2 * (i + 1))
                if i >= 2:
                    d_vo.wait_item(v, B_vo + i - 2)
                v.tensor_copy(vv_sb[pb][:, 0:512], ps_q[0][:])
                s_ve.inc(v.tensor_copy(vv_sb[pb][:, 512:1024], ps_q[1][:]))

    ctx.pop()
    return {}


def emit_ffn0b(ctx, io, tag):
    nc, block = ctx.nc, ctx.block
    NK = 32

    id_c = ctx.const("idg" + tag, IDENT_BF)
    iota_c = ctx.const("iog" + tag, np.ascontiguousarray(np.broadcast_to(
        (16.0 - np.arange(16, dtype=np.float32))[None, None, :],
        (128, 16, 16)).reshape(128, 256)))
    o1c_c = ctx.const("o1cg" + tag, np.ones((128, 1), np.float32))
    o1r_c = ctx.const("o1rg" + tag, np.ones((1, 128), np.float32))

    ctx.push()
    id_sb = ctx.sbuf("idg" + tag, [128, 128], BF16)
    iota_sb = ctx.sbuf("iog" + tag, [128, 256], F32)
    o1c_sb = ctx.sbuf("o1cg" + tag, [128, 1], F32)
    o1r_sb = ctx.sbuf("o1rg" + tag, [1, 128], F32)
    wo_sb = ctx.sbuf("wog" + tag, [128, 8, 1024], BF16)
    bo_sb = ctx.sbuf("bog" + tag, [128, D], F32)
    lng_sb = ctx.sbuf("lngg" + tag, [128, D], F32)
    lnb_sb = ctx.sbuf("lnbg" + tag, [128, D], F32)
    pr_t = [ctx.sbuf("prg" + tag, [128, 256], F32) for _ in range(2)]
    qk_t = [ctx.sbuf("qkg" + tag, [128, 256], F32) for _ in range(2)]
    scrA = ctx.sbuf("scag" + tag, [128, 256], F32)
    scrB = ctx.sbuf("scbg" + tag, [128, 256], F32)
    sc16 = ctx.sbuf("s16g" + tag, [128, 4, 16], F32)
    m_all = ctx.sbuf("mag" + tag, [128, 16, 256], F32)
    e_all = ctx.sbuf("eag" + tag, [128, 16, 16], F32)
    att_sc = ctx.sbuf("atg" + tag, [128, 2, 16], F32)
    z_sb = ctx.sbuf("zg" + tag, [1, 16], F32)
    rz_rep = ctx.sbuf("rzg" + tag, [128, 16], F32)
    w_sc = ctx.sbuf("wscg" + tag, [128, 2, 16], F32)
    wm_sb = ctx.sbuf("wmg" + tag, [128, 2, 256], BF16)
    xv = [ctx.sbuf("xvg" + tag, [128, D], F32) for _ in range(4)]
    vv = [ctx.sbuf("vvg" + tag, [128, D], BF16) for _ in range(4)]
    ub = [ctx.sbuf("ubg" + tag, [128, D], F32) for _ in range(2)]
    avb = [ctx.sbuf("avg" + tag, [128, D], BF16) for _ in range(2)]
    avT = [ctx.sbuf("avtg" + tag, [128, 8, 128], BF16) for _ in range(2)]
    y_sb = [ctx.sbuf("yg" + tag, [128, D], F32) for _ in range(2)]
    yn_sb = [ctx.sbuf("yng" + tag, [128, D], F32) for _ in range(2)]
    yo = [ctx.sbuf("yog" + tag, [128, D], F32) for _ in range(4)]
    bt = ctx.sbuf("btg" + tag, [128, D], F32)
    cnt_sc = ctx.sbuf("cntg" + tag, [128, 1], F32)
    lnu = LNPipe(ctx, "g0lu" + tag)
    lny = LNPipe(ctx, "g0ly" + tag)
    vch = ctx.chain("g0vc" + tag)
    ps_tr = [ctx.psum("pgtr" + tag, [128, 128], BF16) for _ in range(2)]
    ps_o = [ctx.psum("pgo" + tag, [128, 512], F32) for _ in range(2)]
    ps_z = ctx.psum("pgz" + tag, [1, 16], F32)
    ps_bc = ctx.psum("pgbc" + tag, [128, 16], F32)

    s_cst = ctx.sem("gcst")
    d_pg = ctx.dstream("gpg")
    d_qg = ctx.dstream("gqg")
    s_at = ctx.sem("gat")
    s_e = ctx.sem("ge")
    s_zm = ctx.sem("gzm")
    s_zc = ctx.sem("gzc")
    s_bc = ctx.sem("gbc")
    s_wm = ctx.sem("gwm")
    d_xg = ctx.dstream("gxg")
    d_vg = ctx.dstream("gvg")
    d_xo = ctx.dstream("gxo")
    s_u = ctx.sem("gu")
    s_yn = ctx.sem("gyn")
    s_av = ctx.sem("gav")
    s_tr = ctx.sem("gtr")
    s_te = ctx.sem("gte")
    s_om = ctx.sem("gom")
    s_yb = ctx.sem("gyb")
    s_oc = ctx.sem("goc")

    probs_g, qk_g = io["probs_g"], io["qk_g"]
    v_g, x_g, x_out = io["v_g"], io["x_g"], io["x_out"]

    @block.gpsimd
    def _(gp):
        s_cst.inc(gp.dma_start(out=id_sb[:], in_=id_c.ap()), 16)
        s_cst.inc(gp.dma_start(out=iota_sb[:], in_=iota_c.ap()), 16)
        s_cst.inc(gp.dma_start(out=o1c_sb[:], in_=o1c_c.ap()), 16)
        s_cst.inc(gp.dma_start(out=o1r_sb[:], in_=o1r_c.ap()), 16)
        s_cst.inc(gp.dma_start(out=bo_sb[:], in_=io["bo_rep"]), 16)
        s_cst.inc(gp.dma_start(out=lng_sb[:], in_=io["lng_rep"]), 16)
        s_cst.inc(gp.dma_start(out=lnb_sb[:], in_=io["lnb_rep"]), 16)
        WoR = io["WoS"].rearrange("(c p) o -> c p o", p=128)
        for c in range(8):
            s_cst.inc(gp.dma_start(out=wo_sb[:, c, :], in_=WoR[c]), 16)
    CSTN = s_cst.n

    B_pg, B_qg = d_pg.n_issued, d_qg.n_issued
    B_at, B_e, B_zm, B_zc, B_bc, B_wm = s_at.n, s_e.n, s_zm.n, s_zc.n, s_bc.n, s_wm.n
    B_xg, B_vg, B_xo = d_xg.n_issued, d_vg.n_issued, d_xo.n_issued
    B_u, B_yn, B_av, B_tr, B_te = s_u.n, s_yn.n, s_av.n, s_tr.n, s_te.n
    B_om, B_yb, B_oc = s_om.n, s_yb.n, s_oc.n

    @block.sync
    def _(sync):
        sync.wait_ge(s_cst.h, CSTN)
        for i in range(16):
            if i >= 2:
                sync.wait_ge(s_at.h, B_at + i - 1)
            d_pg.issue(sync.dma_start(out=pr_t[i % 2][:], in_=probs_g[i]))
            d_qg.issue(sync.dma_start(out=qk_t[i % 2][:], in_=qk_g[i]))

    @block.vector
    def _(v):
        E = lambda op: vch.edge(v, op)
        v.wait_ge(s_cst.h, CSTN)
        gmaxAP = bass.AP(sc16, 0, [[64, 128], [1, 16], [0, 16]])
        kmaxAP = bass.AP(sc16, 16, [[64, 128], [1, 16], [0, 16]])
        for i in range(16):
            pb = i % 2
            d_pg.wait_item(v, B_pg + i)
            d_qg.wait_item(v, B_qg + i)
            if i >= 2:
                v.wait_ge(s_e.h, B_e + i - 1)
            E(v.reduce_max(sc16[:, 0, :], pr_t[pb][:].rearrange("p (h b) -> p h b", b=16), axis=AX.X))
            E(v.tensor_tensor(scrA[:], pr_t[pb][:], gmaxAP, ALU.subtract))
            E(v.tensor_scalar(scrA[:], scrA[:], 0.0, None, ALU.is_equal))
            E(v.tensor_tensor(scrA[:], scrA[:], iota_sb[:], ALU.mult))
            E(v.reduce_max(sc16[:, 1, :], scrA[:].rearrange("p (h b) -> p h b", b=16), axis=AX.X))
            E(v.tensor_tensor(scrB[:], scrA[:], kmaxAP, ALU.subtract))
            E(v.tensor_scalar(m_all[:, i, :], scrB[:], 0.0, None, ALU.is_equal))
            E(v.tensor_tensor(scrB[:], m_all[:, i, :], qk_t[pb][:], ALU.mult))
            E(v.reduce_sum(att_sc[:, pb, :], scrB[:].rearrange("p (h b) -> p h b", b=16), axis=AX.X))
            s_at.inc(v.engine_nop())
        v.wait_ge(s_zm.h, B_zm + 1)
        E(v.tensor_copy(z_sb[:], ps_z[:]))
        s_zc.inc(v.engine_nop())
        v.wait_ge(s_bc.h, B_bc + 1)
        E(v.reciprocal(rz_rep[:], ps_bc[:]))
        for tau in range(2):
            E(v.tensor_tensor(w_sc[:, tau, :], e_all[:, tau, :], rz_rep[:], ALU.mult))
            wAP = bass.AP(w_sc, tau * 16, [[32, 128], [1, 16], [0, 16]])
            E(v.tensor_tensor(wm_sb[:, tau, :], m_all[:, tau, :], wAP, ALU.mult))
        s_wm.inc(v.engine_nop())

    @block.scalar
    def _(s):
        for i in range(16):
            s.wait_ge(s_at.h, B_at + i + 1)
            s_e.inc(s.activation(e_all[:, i, :], att_sc[:, i % 2, :], AF.Exp, scale=0.125))

    @block.tensor
    def _(t):
        t.wait_ge(s_cst.h, CSTN)
        for i in range(16):
            t.wait_ge(s_e.h, B_e + i + 1)
            mm = t.matmul(ps_z[:], o1c_sb[:], e_all[:, i, :],
                          start=(i == 0), stop=(i == 15))
        s_zm.inc(mm)
        t.wait_ge(s_zc.h, B_zc + 1)
        s_bc.inc(t.matmul(ps_bc[:], o1r_sb[:], z_sb[:], start=True, stop=True))

    @block.sync
    def _(sync):
        for k in range(NK + 4):
            if k < NK:
                tau, b = k // 16, k % 16
                if k >= 4:
                    sync.wait_ge(s_oc.h, B_oc + k - 3)
                d_xg.issue(sync.dma_start(out=xv[k % 4][:], in_=x_g[b, tau * 128:(tau + 1) * 128, :]))
                d_vg.issue(sync.dma_start(out=vv[k % 4][:], in_=v_g[b, tau * 128:(tau + 1) * 128, :]))
            if k >= 4:
                j = k - 4
                tj, bj = j // 16, j % 16
                sync.wait_ge(s_oc.h, B_oc + j + 1)
                d_xo.issue(sync.dma_start(out=x_out[bj, tj * 128:(tj + 1) * 128, :], in_=yo[j % 4][:]))
        d_xo.wait_all(sync)

    @block.scalar
    def _(s):
        for t in range(NK + 3):
            if t < NK:
                d_xg.wait_item(s, B_xg + t)
                lnu.sq(s, t, xv[t % 4][:])
            if 1 <= t < NK + 1:
                j = t - 1
                if j >= 2:
                    s.wait_ge(s_yb.h, B_yb + j - 1)
                lnu.norm(s, j, xv[j % 4][:], ub[j % 2][:], s_u)
            if 2 <= t < NK + 2:
                j2 = t - 2
                s.wait_ge(s_yb.h, B_yb + j2 + 1)
                lny.sq(s, j2, y_sb[j2 % 2][:])
            if 3 <= t < NK + 3:
                j3 = t - 3
                if j3 >= 2:
                    s.wait_ge(s_oc.h, B_oc + j3 - 1)
                lny.norm(s, j3, y_sb[j3 % 2][:], yn_sb[j3 % 2][:], s_yn)

    @block.tensor
    def _(t):
        for k in range(NK):
            t.wait_ge(s_av.h, B_av + k + 1)
            for c in range(8):
                g = 8 * k + c
                if g >= 2:
                    t.wait_ge(s_te.h, B_te + g - 1)
                s_tr.inc(t.transpose(ps_tr[g % 2][:], avb[k % 2][:, c * 128:(c + 1) * 128], id_sb[:]))
            t.wait_ge(s_te.h, B_te + 8 * (k + 1))
            if k >= 1:
                t.wait_ge(s_yb.h, B_yb + k)
            for half in range(2):
                for c in range(8):
                    mm = t.matmul(ps_o[half][:], avT[k % 2][:, c, :],
                                  wo_sb[:, c, half * 512:(half + 1) * 512],
                                  start=(c == 0), stop=(c == 7))
                s_om.inc(mm)

    @block.vector
    def _(v):
        E = lambda op: vch.edge(v, op)
        for t in range(NK + 3):
            if t < NK:
                k = t
                tau, b = k // 16, k % 16
                d_xg.wait_item(v, B_xg + k)
                lnu.stats(v, k, xv[k % 4][:])
                d_vg.wait_item(v, B_vg + k)
                v.wait_ge(s_wm.h, B_wm + 1)
                if k >= 2:
                    v.wait_ge(s_tr.h, B_tr + 8 * (k - 1))
                wmAP = bass.AP(wm_sb, tau * 256 + b, [[512, 128], [16, 16], [0, 64]])
                s_av.inc(v.tensor_tensor(avb[k % 2][:], vv[k % 4][:], wmAP, ALU.mult))
            if 1 <= t < NK + 1:
                k = t - 1
                if k >= 2:
                    v.wait_ge(s_om.h, B_om + 2 * (k - 1))
                for c in range(8):
                    g = 8 * k + c
                    v.wait_ge(s_tr.h, B_tr + g + 1)
                    s_te.inc(v.tensor_copy(avT[k % 2][:, c, :], ps_tr[g % 2][:]))
            if 2 <= t < NK + 2:
                k = t - 2
                tau, b = k // 16, k % 16
                v.wait_ge(s_om.h, B_om + 2 * (k + 1))
                v.wait_ge(s_u.h, B_u + k + 1)
                if k >= 2:
                    v.wait_ge(s_yn.h, B_yn + k - 1)
                mAP = bass.AP(m_all, tau * 256 + b, [[4096, 128], [16, 16]])
                E(v.reduce_sum(cnt_sc[:], mAP, axis=AX.X))
                v.tensor_scalar_mul(bt[:], bo_sb[:], cnt_sc[:])
                v.tensor_tensor(y_sb[k % 2][:, 0:512], ub[k % 2][:, 0:512], ps_o[0][:], ALU.add)
                v.tensor_tensor(y_sb[k % 2][:, 512:1024], ub[k % 2][:, 512:1024], ps_o[1][:], ALU.add)
                s_yb.inc(v.tensor_tensor(y_sb[k % 2][:], y_sb[k % 2][:], bt[:], ALU.add))
                lny.stats(v, k, y_sb[k % 2][:])
            if 3 <= t < NK + 3:
                k = t - 3
                v.wait_ge(s_yn.h, B_yn + k + 1)
                if k >= 4:
                    d_xo.wait_item(v, B_xo + k - 4)
                v.tensor_tensor(yn_sb[k % 2][:], yn_sb[k % 2][:], lng_sb[:], ALU.mult)
                v.tensor_tensor(yn_sb[k % 2][:], yn_sb[k % 2][:], lnb_sb[:], ALU.add)
                v.tensor_scalar_mul(yn_sb[k % 2][:], yn_sb[k % 2][:], 0.5)
                s_oc.inc(v.tensor_tensor(yo[k % 4][:], xv[k % 4][:], yn_sb[k % 2][:], ALU.add))

    ctx.pop()
    return {}



MOE_IO_KEYS = ("W1", "b1cols", "W2", "b2rows", "gateW", "sharedWb", "gam_rep", "bet_rep")


def build_moe_program():
    nc = bass.Bass("TRN2", target_bir_lowering=False, detect_race_conditions=False)
    io = {}
    io["x_in"] = nc.dram_tensor("x_in", [4096, D], F32, kind="ExternalInput").ap()
    io["x_out"] = nc.dram_tensor("x_out", [4096, D], F32, kind="ExternalOutput").ap()
    io["W1"] = nc.dram_tensor("W1", [8, 1024, 4096], F32, kind="ExternalInput").ap()
    io["b1cols"] = nc.dram_tensor("b1cols", [8, 128, 32], F32, kind="ExternalInput").ap()
    io["W2"] = nc.dram_tensor("W2", [8, 4096, 1024], F32, kind="ExternalInput").ap()
    io["b2rows"] = nc.dram_tensor("b2rows", [8, 1, 1024], F32, kind="ExternalInput").ap()
    io["gateW"] = nc.dram_tensor("gateW", [1024, 8], F32, kind="ExternalInput").ap()
    io["sharedWb"] = nc.dram_tensor("sharedWb", [1025, 1024], F32, kind="ExternalInput").ap()
    io["gam_rep"] = nc.dram_tensor("gam_rep", [128, 1024], F32, kind="ExternalInput").ap()
    io["bet_rep"] = nc.dram_tensor("bet_rep", [128, 1024], F32, kind="ExternalInput").ap()
    io["uT"] = nc.dram_tensor("uT_scr", [8, 128, 4096], BF16, kind=("ExternalOutput" if DBG_UT else "Internal")).ap()
    io["c_dbg"] = nc.dram_tensor("c_dbg", [128, 32, 8], F32, kind="ExternalOutput").ap()
    io["outacc"] = nc.dram_tensor("outacc_scr", [4096, 1024], F32).ap()
    es = ExitStack()
    with nc.Block() as block:
        ctx = Ctx(nc, block, es)
        emit_moe(ctx, io, "m")
    es.close()
    return nc


def build_ffn0a_program(NB=2):
    nc = bass.Bass("TRN2", target_bir_lowering=False, detect_race_conditions=False)
    io = {"n_batches": NB}
    io["x_in"] = nc.dram_tensor("x_in", [NB * 2048, 1024], F32, kind="ExternalInput").ap()
    io["Wg"] = nc.dram_tensor("Wg", [1024, 16], F32, kind="ExternalInput").ap()
    io["WqcTb"] = nc.dram_tensor("WqcTb", [1025, 1024], F32, kind="ExternalInput").ap()
    io["WkvcTb"] = nc.dram_tensor("WkvcTb", [1025, 2048], F32, kind="ExternalInput").ap()
    io["probs_out"] = nc.dram_tensor("probs_out", [NB, 2048, 16], F32, kind="ExternalOutput").ap()
    io["qk_out"] = nc.dram_tensor("qk_out", [NB, 2048, 16], F32, kind="ExternalOutput").ap()
    io["v_out"] = nc.dram_tensor("v_out", [NB, 2048, 1024], BF16, kind="ExternalOutput").ap()
    es = ExitStack()
    with nc.Block() as block:
        emit_ffn0a(Ctx(nc, block, es), io, "z")
    es.close()
    return nc


def build_ffn0b_program():
    nc = bass.Bass("TRN2", target_bir_lowering=False, detect_race_conditions=False)
    io = {}
    io["probs_g"] = nc.dram_tensor("probs_g", [16, 128, 256], F32, kind="ExternalInput").ap()
    io["qk_g"] = nc.dram_tensor("qk_g", [16, 128, 256], F32, kind="ExternalInput").ap()
    io["v_g"] = nc.dram_tensor("v_g", [16, 256, 1024], BF16, kind="ExternalInput").ap()
    io["x_g"] = nc.dram_tensor("x_g", [16, 256, 1024], F32, kind="ExternalInput").ap()
    io["WoS"] = nc.dram_tensor("WoS", [1024, 1024], F32, kind="ExternalInput").ap()
    io["bo_rep"] = nc.dram_tensor("bo_rep", [128, 1024], F32, kind="ExternalInput").ap()
    io["lng_rep"] = nc.dram_tensor("lng_rep", [128, 1024], F32, kind="ExternalInput").ap()
    io["lnb_rep"] = nc.dram_tensor("lnb_rep", [128, 1024], F32, kind="ExternalInput").ap()
    io["x_out"] = nc.dram_tensor("x_out", [16, 256, 1024], F32, kind="ExternalOutput").ap()
    es = ExitStack()
    with nc.Block() as block:
        emit_ffn0b(Ctx(nc, block, es), io, "z")
    es.close()
    return nc


def build_attn0_program():
    nc = bass.Bass("TRN2", target_bir_lowering=False, detect_race_conditions=False)
    io = {}
    io["x_in"] = nc.dram_tensor("x_in", [4096, 1024], F32, kind="ExternalInput").ap()
    io["x_out"] = nc.dram_tensor("x_out", [4096, 1024], F32, kind="ExternalOutput").ap()
    io["WqkvT"] = nc.dram_tensor("WqkvT", [1024, 3072], F32, kind="ExternalInput").ap()
    io["WprojTb"] = nc.dram_tensor("WprojTb", [1025, 1024], F32, kind="ExternalInput").ap()
    io["kv_scr"] = nc.dram_tensor("kv_scr", [64, 8 * 65], BF16).ap().rearrange("p (h e) -> p h e", e=65)
    es = ExitStack()
    with nc.Block() as block:
        emit_attn0(Ctx(nc, block, es), io, "z")
    es.close()
    return nc


def build_attn1_program():
    nc = bass.Bass("TRN2", target_bir_lowering=False, detect_race_conditions=False)
    io = {}
    io["x_in"] = nc.dram_tensor("x_in", [4096, 1024], F32, kind="ExternalInput").ap()
    io["x_out"] = nc.dram_tensor("x_out", [4096, 1024], F32, kind="ExternalOutput").ap()
    for nm, shp in [("WqTb", (1025, 2048)), ("WkTb", (1025, 2048)), ("WvTb", (1025, 1024)),
                    ("WoT", (1024, 1024)), ("bo_row", (1, 1024)), ("hnw_col", (1024, 1)),
                    ("lq1T", (64, 16)), ("lk1T", (64, 16)), ("lq2T", (64, 16)), ("lk2T", (64, 16))]:
        io[nm] = nc.dram_tensor(nm, list(shp), F32, kind="ExternalInput").ap()
    io["aout_scr"] = nc.dram_tensor("aout_scr", [2, 2048, 1024], BF16).ap()
    es = ExitStack()
    with nc.Block() as block:
        emit_attn1(Ctx(nc, block, es), io, "a")
    es.close()
    return nc


def build_moe_program2():
    nc = bass.Bass("TRN2", target_bir_lowering=False, detect_race_conditions=False)
    io = {}
    io["x_in"] = nc.dram_tensor("x_in", [4096, D], F32, kind="ExternalInput").ap()
    io["x_out"] = nc.dram_tensor("x_out", [4096, D], F32, kind="ExternalOutput").ap()
    io["W1"] = nc.dram_tensor("W1", [8, 1024, 4096], F32, kind="ExternalInput").ap()
    io["b1cols"] = nc.dram_tensor("b1cols", [8, 128, 32], F32, kind="ExternalInput").ap()
    io["W2"] = nc.dram_tensor("W2", [8, 4096, 1024], F32, kind="ExternalInput").ap()
    io["b2rows"] = nc.dram_tensor("b2rows", [8, 1, 1024], F32, kind="ExternalInput").ap()
    io["gateW"] = nc.dram_tensor("gateW", [1024, 8], F32, kind="ExternalInput").ap()
    io["sharedWb"] = nc.dram_tensor("sharedWb", [1025, 1024], F32, kind="ExternalInput").ap()
    io["gam_rep"] = nc.dram_tensor("gam_rep", [128, 1024], F32, kind="ExternalInput").ap()
    io["bet_rep"] = nc.dram_tensor("bet_rep", [128, 1024], F32, kind="ExternalInput").ap()
    io["uT"] = nc.dram_tensor("uT_scr", [8, 128, 4096], BF16, kind="Internal").ap()
    io["outacc"] = nc.dram_tensor("outacc_scr", [4096, 1024], F32).ap()
    es = ExitStack()
    with nc.Block() as block:
        emit_moe(Ctx(nc, block, es), io, "m")
    es.close()
    return nc


_PROGS = {}


def _prog(name, builder):
    if name not in _PROGS:
        _PROGS[name] = builder()
    return _PROGS[name]


def kernel(**inputs):
    hp = host_prepare(inputs)
    x = np.ascontiguousarray(np.asarray(inputs["L0_attn_x"], np.float32))
    CORES = list(range(NCORES))

    wA = {"Wg": hp["f0_Wg"], "WqcTb": hp["f0_WqcTb"], "WkvcTb": hp["f0_WkvcTb"]}
    wB = {"WoS": hp["f0_WoS"], "bo_rep": hp["f0_bo_rep"],
          "lng_rep": hp["f0_lng_rep"], "lnb_rep": hp["f0_lnb_rep"]}
    wC = {"WqkvT": hp["a0_WqkvT"], "WprojTb": hp["a0_WprojTb"]}
    wD = {"W1": hp["f1_W1"], "b1cols": hp["f1_b1cols"], "W2": hp["f1_W2"],
          "b2rows": hp["f1_b2rows"], "gateW": hp["f1_gateW"],
          "sharedWb": hp["f1_sharedWb"], "gam_rep": hp["f1_gam_rep"],
          "bet_rep": hp["f1_bet_rep"]}
    wE = {"WqTb": hp["a1_WqTb"], "WkTb": hp["a1_WkTb"], "WvTb": hp["a1_WvTb"],
          "WoT": hp["a1_WoT"], "bo_row": hp["a1_bo_row"], "hnw_col": hp["a1_hnw_col"],
          "lq1T": hp["a1_lq1T"], "lk1T": hp["a1_lk1T"],
          "lq2T": hp["a1_lq2T"], "lk2T": hp["a1_lk2T"]}

    def run_batchwise(nc, wmap, xcur):
        maps = [dict(wmap, x_in=np.ascontiguousarray(xcur[2 * c:2 * c + 2].reshape(4096, 1024)))
                for c in CORES]
        res = run_bass_kernel_spmd(nc, maps, core_ids=CORES)
        out = np.empty_like(xcur)
        for c in CORES:
            out[2 * c:2 * c + 2] = np.asarray(res.results[c]["x_out"], np.float32).reshape(2, 2048, 1024)
        return out

    def run_ffn0(xcur):
        ncA = _prog("A", build_ffn0a_program)
        maps = [dict(wA, x_in=np.ascontiguousarray(xcur[2 * c:2 * c + 2].reshape(4096, 1024)))
                for c in CORES]
        res = run_bass_kernel_spmd(ncA, maps, core_ids=CORES)
        probs = np.concatenate([np.asarray(res.results[c]["probs_out"]).reshape(2, 2048, 16) for c in CORES], 0)
        qk = np.concatenate([np.asarray(res.results[c]["qk_out"]).reshape(2, 2048, 16) for c in CORES], 0)
        vv = np.concatenate([np.asarray(res.results[c]["v_out"]).reshape(2, 2048, 1024) for c in CORES], 0)
        pg_full = np.ascontiguousarray(probs.astype(np.float32).transpose(1, 2, 0).reshape(16, 128, 256))
        qg_full = np.ascontiguousarray(qk.astype(np.float32).transpose(1, 2, 0).reshape(16, 128, 256))
        ncB = _prog("B", build_ffn0b_program)
        maps = []
        for c in CORES:
            sl = slice(c * 256, (c + 1) * 256)
            maps.append(dict(wB,
                             probs_g=np.ascontiguousarray(np.roll(pg_full, -2 * c, axis=0)),
                             qk_g=np.ascontiguousarray(np.roll(qg_full, -2 * c, axis=0)),
                             v_g=np.ascontiguousarray(vv[:, sl, :]),
                             x_g=np.ascontiguousarray(xcur[:, sl, :])))
        res = run_bass_kernel_spmd(ncB, maps, core_ids=CORES)
        out = np.empty_like(xcur)
        for c in CORES:
            sl = slice(c * 256, (c + 1) * 256)
            out[:, sl, :] = np.asarray(res.results[c]["x_out"], np.float32).reshape(16, 256, 1024)
        return out

    x = run_ffn0(x)
    x = run_batchwise(_prog("C", build_attn0_program), wC, x)
    x = run_ffn0(x)
    x = run_batchwise(_prog("D", build_moe_program2), wD, x)
    x = run_batchwise(_prog("E", build_attn1_program), wE, x)
    x = run_batchwise(_prog("D", build_moe_program2), wD, x)
    return x
